# version 22
# baseline (speedup 1.0000x reference)
import contextlib
import numpy as np
import ml_dtypes
import concourse.bass as bass
import concourse.mybir as mybir
from concourse.bass_utils import run_bass_kernel_spmd

F32 = mybir.dt.float32
BF16 = mybir.dt.bfloat16
I8 = mybir.dt.int8
ALU = mybir.AluOpType
ACT = mybir.ActivationFunctionType
AX = mybir.AxisListType

D = 2048
ND = 16
T = 2048
TM = 512
NMT = T // TM
HID = 5632
NJ = HID // 128
CLIP_W = 1.4999999
EPS = 1e-5
QMAX = 127.0
NSLAB = 272


def build():
    nc = bass.Bass("TRN2", target_bir_lowering=False)
    xT = nc.dram_tensor("xT", [ND * NMT * 128, TM], F32, kind="ExternalInput").ap()
    Wq = nc.dram_tensor("Wq", [2 * ND * 128, 1024], F32, kind="ExternalInput").ap()
    Wk = nc.dram_tensor("Wk", [2 * ND * 128, 1024], F32, kind="ExternalInput").ap()
    Wv = nc.dram_tensor("Wv", [2 * ND * 128, 1024], F32, kind="ExternalInput").ap()
    Wo = nc.dram_tensor("Wo", [2 * ND * 128, 1024], F32, kind="ExternalInput").ap()
    w1T = nc.dram_tensor("w1T", [(NJ // 2) * 4 * 128, 1024], F32, kind="ExternalInput").ap()
    w2T = nc.dram_tensor("w2T", [(NJ // 2) * 4 * 128, 1024], F32, kind="ExternalInput").ap()
    w3T = nc.dram_tensor("w3T", [2 * NJ * 128, 1024], F32, kind="ExternalInput").ap()
    cf = nc.dram_tensor("cf", [128, 392], F32, kind="ExternalInput").ap()
    cbm = nc.dram_tensor("cbm", [128, 128], BF16, kind="ExternalInput").ap()
    cmask = nc.dram_tensor("cmask", [128, 512], BF16, kind="ExternalInput").ap()
    outT = nc.dram_tensor("outT", [ND * NMT * 128, TM], F32, kind="ExternalOutput").ap()
    Gh = nc.dram_tensor("Gh", [NJ * NMT * 128, TM], F32, kind="Internal").ap()
    Qh8 = nc.dram_tensor("Qh8", [HID, TM], I8, kind="Internal").ap()
    Wqc = nc.dram_tensor("Wqc", [2 * ND * 128, 1024], BF16, kind="Internal").ap()
    Wkc = nc.dram_tensor("Wkc", [2 * ND * 128, 1024], BF16, kind="Internal").ap()
    Wvc = nc.dram_tensor("Wvc", [2 * ND * 128, 1024], BF16, kind="Internal").ap()
    Woc = nc.dram_tensor("Woc", [2 * ND * 128, 1024], BF16, kind="Internal").ap()
    wq1c = nc.dram_tensor("wq1c", [(NJ // 2) * 4 * 128, 1024], I8, kind="Internal").ap()
    wq2c = nc.dram_tensor("wq2c", [(NJ // 2) * 4 * 128, 1024], I8, kind="Internal").ap()
    wq3c = nc.dram_tensor("wq3c", [2 * NJ * 128, 1024], I8, kind="Internal").ap()

    ops = {k: [] for k in ("sync", "gpsimd", "vector", "scalar", "tensor")}
    C = {"t": 0, "v": 0, "s": 0}
    SEM = {}
    STREAMS = {}

    def em(eng, fn):
        ops[eng].append(fn)

    def wt(eng, sem, val):
        assert val <= C[sem], f"wait-on-future {sem}: {val} > {C[sem]}"
        if val <= 0:
            return
        em(eng, lambda e, s=sem, v=val: e.wait_ge(SEM[s], v))

    def vi(fn):
        C["v"] += 1
        em("vector", lambda e, f=fn: f(e).then_inc(SEM["v"], 1))

    def vw():
        wt("vector", "v", C["v"])

    def si(fn):
        C["s"] += 1
        em("scalar", lambda e, f=fn: f(e).then_inc(SEM["s"], 1))

    def sw():
        wt("scalar", "s", C["s"])

    def sdma(queue, stream, slot, out, in_):
        sems, cnt = STREAMS[stream]
        cnt[slot] += 16
        em(queue, lambda e, o=out, i=in_, sm=sems[slot]: e.dma_start(
            out=o, in_=i).then_inc(sm, 16))
        return (stream, slot, cnt[slot])

    def swait(eng, tok):
        if tok is None:
            return
        stream, slot, v = tok
        sems, _ = STREAMS[stream]
        em(eng, lambda e, sm=sems[slot], v_=v: e.wait_ge(sm, v_))

    def slast(stream, slot):
        sems, cnt = STREAMS[stream]
        return (stream, slot, cnt[slot]) if cnt[slot] > 0 else None

    with contextlib.ExitStack() as _st:
        def SB(name, shape, dt):
            return _st.enter_context(nc.sbuf_tensor(name, shape, dt))

        def PS(name, shape):
            return _st.enter_context(nc.psum_tensor(name, shape, F32))

        xs = SB("xs", [128, ND * TM], F32)
        Ab = SB("Ab", [128, ND * TM], BF16)
        Qb = SB("Qb", [128, ND * TM], BF16)
        Kb = SB("Kb", [128, ND * TM], BF16)
        Vb = SB("Vb", [128, ND * TM], BF16)
        ob = SB("ob", [128, ND * TM], BF16)
        Sbd = SB("Sbd", [128, 2 * 512], BF16)
        VTs = SB("VTs", [128, 2 * 512], BF16)
        otm = SB("otm", [128, 2 * 512], BF16)
        msk = SB("msk", [128, 512], BF16)
        qHb = SB("qHb", [128, 3 * 512], BF16)
        tmp1 = SB("tmp1", [128, 1024], F32)
        acc = SB("acc", [128, TM], F32)
        acc2 = SB("acc2", [128, TM], F32)
        bc0 = SB("bc0", [128, TM], F32)
        bc1 = SB("bc1", [128, TM], F32)
        rows = SB("rows", [1, 2048], F32)
        tmQ = SB("tmQ", [128, 64], F32)
        i8t = SB("i8t", [128, 3 * 512], I8)
        wsl = SB("wsl", [128, 4 * 1024], BF16)
        w3s = SB("w3s", [128, 2 * 2048], BF16)
        w3i = SB("w3i", [128, 4 * 1024], I8)
        wfB = SB("wfB", [128, 2 * 1024], F32)
        wi8 = SB("wi8", [128, 1024], I8)
        pcb = SB("pcb", [128, 1024], BF16)
        wb1 = SB("wb1", [128, 3 * 1024], BF16)
        wb2 = SB("wb2", [128, 3 * 1024], BF16)
        wb1i = SB("wb1i", [128, 3 * 1024], I8)
        wb2i = SB("wb2i", [128, 3 * 1024], I8)
        Gbf = SB("Gbf", [128, 2 * 512], F32)
        accW = SB("accW", [128, NSLAB], F32)
        csf = SB("csf", [128, 392], F32)
        csb = SB("csb", [128, 128], BF16)
        psA0 = PS("psA0", [128, 512])
        psA1 = PS("psA1", [128, 512])
        psB0 = PS("psB0", [128, 512])
        psB1 = PS("psB1", [128, 512])
        psS = PS("psS", [128, 512])
        psV0 = PS("psV0", [128, 512])
        psV1 = PS("psV1", [128, 512])
        psR = PS("psR", [128, 512])
        sem_x = _st.enter_context(nc.semaphore("sem_x"))
        sem_w = _st.enter_context(nc.semaphore("sem_w"))
        sem_d = _st.enter_context(nc.semaphore("sem_d"))
        sem_t = _st.enter_context(nc.semaphore("sem_t"))
        sem_v = _st.enter_context(nc.semaphore("sem_v"))
        sem_s = _st.enter_context(nc.semaphore("sem_s"))
        block = _st.enter_context(nc.Block())
        for _nm, _ns in (("xl", 1), ("st", 2), ("awf", 2), ("awr", 4), ("awc", 4), ("wfB", 2),
                         ("w1", 3), ("w2", 3), ("w3s", 4),
                         ("gl", 2), ("gs", 2), ("qs", 3), ("qr", 3), ("c1", 3), ("c2", 3), ("c3", 4),
                         ("out", 1)):
            STREAMS[_nm] = ([_st.enter_context(nc.semaphore(f"sm_{_nm}{_i}"))
                             for _i in range(_ns)], [0] * _ns)
        SEM.update(x=sem_x, w=sem_w, d=sem_d, t=sem_t, v=sem_v, s=sem_s)
        psM = [psA0, psA1, psB0, psB1]
        psM8v = [psA0, psA1, psB0, psB1, psS, psV0, psV1, psR]

        ID_F = csf[:, 0:128]
        ONE_COL = csf[:, 128:129]
        ONE_ROW = csf[0:1, 129:257]
        NEG_ROW = csf[0:1, 257:385]
        ONE_1 = csf[0:1, 128:129]
        ID_B = csb[:, 0:128]
        MU = rows[:, 0:512]
        RS = rows[:, 512:1024]
        WA = rows[:, 1024:1536]
        WB = rows[:, 1536:2048]

        prod_units = []
        prod_done = [0]
        PT = {}
        PIDX = {}
        pcb_guard = [None]
        wi8_guard = [None]
        pfi = [0]

        def drain_P(n):
            while n > 0 and prod_done[0] < len(prod_units):
                prod_units[prod_done[0]]()
                prod_done[0] += 1
                n -= 1

        def need_P(key):
            idx = PIDX[key]
            while prod_done[0] <= idx:
                prod_units[prod_done[0]]()
                prod_done[0] += 1

        def mk_cast_unit(wname, wap, cache_ap, dgrp, dinc):
            def unit():
                fb = pfi[0] % 2
                pfi[0] += 1
                blk = dgrp * ND + dinc
                fdst = wfB[:, fb * 1024:(fb + 1) * 1024]
                gsem, gval = wfB_guard[fb]
                if gval > 0:
                    wt("gpsimd", gsem, gval)
                ftok = sdma("gpsimd", "awf", fb, fdst,
                            wap[blk * 128:(blk + 1) * 128, :])
                swait("scalar", ftok)
                swait("scalar", pcb_guard[0])
                si(lambda e, d=pcb[:, 0:1024], f=fdst: e.copy(
                    out=d, in_=f))
                wfB_guard[fb] = ("s", C["s"])
                sw()
                PT[(wname, dgrp, dinc)] = sdma(
                    "scalar", "awc", fb,
                    cache_ap[blk * 128:(blk + 1) * 128, :],
                    pcb[:, 0:1024])
                pcb_guard[0] = PT[(wname, dgrp, dinc)]
            PIDX[(wname, dgrp, dinc)] = len(prod_units)
            prod_units.append(unit)

        for _w, _wap, _cap in (("Wq", Wq, Wqc), ("Wk", Wk, Wkc),
                               ("Wv", Wv, Wvc), ("Wo", Wo, Woc)):
            for _dg in range(2):
                for _di in range(ND):
                    mk_cast_unit(_w, _wap, _cap, _dg, _di)

        stats_ranges = {}
        _scol = [0]

        def mk_stats(tag, wap, nrow, ncol):
            c0 = _scol[0]
            for r in range(nrow // 128):
                for cs in range(0, ncol, 1024):
                    cw = min(1024, ncol - cs)
                    ci = _scol[0]
                    _scol[0] += 1

                    def unit(r_=r, cs_=cs, cw_=cw, w_=wap, ci_=ci):
                        fb = pfi[0] % 2
                        pfi[0] += 1
                        dst = wfB[:, fb * 1024: fb * 1024 + cw_]
                        gsem, gval = wfB_guard[fb]
                        if gval > 0:
                            wt("gpsimd", gsem, gval)
                        tok = sdma("gpsimd", "awf", fb, dst,
                                   w_[r_ * 128:(r_ + 1) * 128, cs_:cs_ + cw_])
                        swait("scalar", tok)
                        sw()
                        em("scalar", lambda e, d=dst, c2=ci_: e.activation(
                            out=d, in_=d, func=ACT.Abs,
                            accum_out=accW[:, c2:c2 + 1]).then_inc(sem_s, 1))
                        C["s"] += 1
                        wfB_guard[fb] = ("s", C["s"])
                    prod_units.append(unit)
            stats_ranges[tag] = (c0, _scol[0])

        mk_stats("w1", w1T, (NJ // 2) * 4 * 128, 1024)
        mk_stats("w2", w2T, (NJ // 2) * 4 * 128, 1024)
        mk_stats("w3", w3T, 2 * NJ * 128, 1024)
        assert _scol[0] <= NSLAB

        def emit_wscale(tag, wi, nelem):
            c0, c1 = stats_ranges[tag]
            wt("vector", "s", C["s"])
            vw()
            vi(lambda e: e.tensor_reduce(
                out=tmQ[:, 32 + wi:33 + wi], in_=accW[:, c0:c1], axis=AX.X,
                op=ALU.add))
            wt("tensor", "v", C["v"])
            wt("tensor", "s", C["s"])
            em("tensor", lambda e: e.matmul(
                out=psR[0:1, 480 + wi:481 + wi], lhsT=ONE_COL,
                rhs=tmQ[:, 32 + wi:33 + wi], start=True, stop=True).then_inc(sem_t, 1))
            C["t"] += 1
            wt("vector", "t", C["t"])
            vw()
            vi(lambda e: e.tensor_scalar(
                out=tmQ[0:1, 8 + wi:9 + wi], in0=psR[0:1, 480 + wi:481 + wi],
                scalar1=1.0 / nelem, scalar2=EPS, op0=ALU.mult, op1=ALU.max))
            vw()
            vi(lambda e: e.reciprocal(
                out=tmQ[0:1, 11 + wi:12 + wi], in_=tmQ[0:1, 8 + wi:9 + wi]))
            wt("tensor", "v", C["v"])
            em("tensor", lambda e: e.matmul(
                out=psR[:, 488 + 2 * wi:489 + 2 * wi], lhsT=ONE_ROW,
                rhs=tmQ[0:1, 11 + wi:12 + wi], start=True, stop=True))
            em("tensor", lambda e: e.matmul(
                out=psR[:, 489 + 2 * wi:490 + 2 * wi], lhsT=ONE_ROW,
                rhs=tmQ[0:1, 8 + wi:9 + wi], start=True, stop=True).then_inc(sem_t, 1))
            C["t"] += 1
            wt("scalar", "t", C["t"])
            em("scalar", lambda e: e.copy(out=tmQ[:, 16 + wi:17 + wi],
                                          in_=psR[:, 488 + 2 * wi:489 + 2 * wi]))
            em("scalar", lambda e: e.copy(out=tmQ[:, 24 + wi:25 + wi],
                                          in_=psR[:, 489 + 2 * wi:490 + 2 * wi]).then_inc(sem_s, 1))
            C["s"] += 1

        PIDX[("ws", 0)] = len(prod_units)
        prod_units.append(lambda: emit_wscale("w1", 0, D * HID))
        PIDX[("ws", 1)] = len(prod_units)
        prod_units.append(lambda: emit_wscale("w2", 1, D * HID))
        PIDX[("ws", 2)] = len(prod_units)
        prod_units.append(lambda: emit_wscale("w3", 2, HID * D))

        def mk_quad_quant(wname, wap, cache_ap, cstream, p, q, wi):
            def unit():
                fb = pfi[0] % 2
                pfi[0] += 1
                blk = p * 4 + q
                fdst = wfB[:, fb * 1024:(fb + 1) * 1024]
                gsem, gval = wfB_guard[fb]
                if gval > 0:
                    wt("gpsimd", gsem, gval)
                ftok = sdma("gpsimd", "awf", fb, fdst,
                            wap[blk * 128:(blk + 1) * 128, :])
                swait("vector", ftok)
                wt("vector", "s", C["s"])
                vw()
                vi(lambda e, f=fdst: e.tensor_scalar(
                    out=f, in0=f, scalar1=tmQ[:, 16 + wi:17 + wi], scalar2=CLIP_W,
                    op0=ALU.mult, op1=ALU.min))
                vw()
                swait("vector", wi8_guard[0])
                vi(lambda e, f=fdst: e.tensor_scalar(
                    out=wi8[:], in0=f, scalar1=-CLIP_W, scalar2=None, op0=ALU.max))
                wfB_guard[fb] = ("v", C["v"])
                wt("scalar", "v", C["v"])
                PT[(wname, p, q)] = sdma(
                    "scalar", cstream, fb,
                    cache_ap[blk * 128:(blk + 1) * 128, :], wi8[:])
                wi8_guard[0] = PT[(wname, p, q)]
            PIDX[(wname, p, q)] = len(prod_units)
            prod_units.append(unit)

        def mk_w3_quant(mh, jc):
            def unit():
                fb = pfi[0] % 2
                pfi[0] += 1
                fdst = wfB[:, fb * 1024:(fb + 1) * 1024]
                gsem, gval = wfB_guard[fb]
                if gval > 0:
                    wt("gpsimd", gsem, gval)
                blk = mh * NJ + jc
                ftok = sdma("gpsimd", "awf", fb, fdst,
                            w3T[blk * 128:(blk + 1) * 128, :])
                swait("vector", ftok)
                wt("vector", "s", C["s"])
                vw()
                vi(lambda e, f=fdst: e.tensor_scalar(
                    out=f, in0=f, scalar1=tmQ[:, 18:19], scalar2=CLIP_W,
                    op0=ALU.mult, op1=ALU.min))
                vw()
                swait("vector", wi8_guard[0])
                vi(lambda e, f=fdst: e.tensor_scalar(
                    out=wi8[:], in0=f, scalar1=-CLIP_W, scalar2=None, op0=ALU.max))
                wfB_guard[fb] = ("v", C["v"])
                wt("scalar", "v", C["v"])
                PT[("w3", mh, jc)] = sdma(
                    "scalar", "c3", fb,
                    wq3c[blk * 128:(blk + 1) * 128, :], wi8[:])
                wi8_guard[0] = PT[("w3", mh, jc)]
            PIDX[("w3", mh, jc)] = len(prod_units)
            prod_units.append(unit)

        for _p in range(NJ // 2):
            for _q in range(4):
                mk_quad_quant("w1", w1T, wq1c, "c1", _p, _q, 0)
                mk_quad_quant("w2", w2T, wq2c, "c2", _p, _q, 1)
        for _mh in range(2):
            for _jc in range(NJ):
                mk_w3_quant(_mh, _jc)

        def ln_stats(ready=None):
            wt("tensor", "s", C["s"])
            if ready is None:
                wt("tensor", "v", C["v"])
                wt("scalar", "v", C["v"])
            sq_done = [0]
            for pr in range(ND // 2):
                c0, c1 = 2 * pr, 2 * pr + 1
                wt("scalar", "t", sq_done[-1])
                if ready is not None:
                    wt("scalar", "v", max(ready[c0], ready[c1]))
                em("scalar", lambda e, c_=c0: e.activation(
                    out=tmp1[:], in_=xs[:, c_ * 512:(c_ + 2) * 512],
                    func=ACT.Square).then_inc(sem_s, 1))
                C["s"] += 1
                wt("tensor", "s", C["s"])
                if ready is not None:
                    wt("tensor", "v", max(ready[c0], ready[c1],
                                          ready[8] if pr == 0 else 0))
                st, sp = (pr == 0), (pr == ND // 2 - 1)

                def lnmm(e, c0_=c0, c1_=c1, st_=st, sp_=sp):
                    e.matmul(out=psR[0:1, 0:512], lhsT=ONE_COL,
                             rhs=xs[:, c0_ * 512:(c0_ + 1) * 512], start=st_, stop=False)
                    e.matmul(out=psR[0:1, 0:512], lhsT=ONE_COL,
                             rhs=xs[:, c1_ * 512:(c1_ + 1) * 512], start=False, stop=sp_)
                    e.matmul(out=psR[32:33, 0:512], lhsT=ONE_COL,
                             rhs=tmp1[:, 0:512], start=st_, stop=False,
                             tile_position=(0, 32))
                    e.matmul(out=psR[32:33, 0:512], lhsT=ONE_COL,
                             rhs=tmp1[:, 512:1024], start=False, stop=sp_,
                             tile_position=(0, 32)).then_inc(sem_t, 1)
                em("tensor", lnmm)
                C["t"] += 1
                sq_done.append(C["t"])
                drain_P(2)
            wt("vector", "t", C["t"])
            vw()
            vi(lambda e: e.tensor_single_scalar(
                out=MU, in_=psR[0:1, 0:512], scalar=1.0 / D, op=ALU.mult))
            vi(lambda e: e.tensor_single_scalar(
                out=WA, in_=psR[32:33, 0:512], scalar=1.0 / D, op=ALU.mult))
            vw()
            vi(lambda e: e.tensor_tensor(out=WB, in0=MU, in1=MU, op=ALU.mult))
            vw()
            vi(lambda e: e.tensor_tensor(out=WA, in0=WA, in1=WB, op=ALU.subtract))
            wt("scalar", "v", C["v"])
            sw()
            si(lambda e: e.activation(
                out=WB, in_=WA, func=ACT.Sqrt, bias=csf[0:1, 385:386]))
            wt("vector", "s", C["s"])
            vw()
            vi(lambda e: e.reciprocal(out=RS, in_=WB))

        def bcast_row(src_row, dst_ps, lhsT_row):
            em("tensor", lambda e, s=src_row, d=dst_ps, l=lhsT_row: e.matmul(
                out=d, lhsT=l, rhs=s, start=True, stop=True).then_inc(sem_t, 1))
            C["t"] += 1

        def ln1():
            ln_stats()
            wt("vector", "s", C["s"])
            vw()
            vi(lambda e: e.tensor_tensor(out=WB, in0=MU, in1=RS, op=ALU.mult))
            wt("tensor", "v", C["v"])
            wt("tensor", "s", C["s"])
            bcast_row(RS, psV0[:, 0:512], ONE_ROW)
            bcast_row(WB, psV1[:, 0:512], NEG_ROW)
            wt("scalar", "t", C["t"])
            wt("scalar", "v", C["v"])
            em("scalar", lambda e: e.copy(out=bc0[:], in_=psV0[:, 0:512]))
            em("scalar", lambda e: e.copy(out=bc1[:], in_=psV1[:, 0:512]).then_inc(sem_s, 1))
            C["s"] += 1
            wt("vector", "s", C["s"])
            for c in range(ND):
                vw()
                vi(lambda e, c_=c: e.tensor_tensor(
                    out=tmp1[:, 0:512], in0=xs[:, c_ * 512:(c_ + 1) * 512],
                    in1=bc0[:], op=ALU.mult))
                vw()
                vi(lambda e, c_=c: e.tensor_tensor(
                    out=Ab[:, c_ * 512:(c_ + 1) * 512], in0=tmp1[:, 0:512],
                    in1=bc1[:], op=ALU.add))

        wsl_busy = [0, 0, 0, 0]
        wfB_guard = [("s", 0), ("s", 0)]
        wsl_c = [None, None, None, None]
        wsl_i = [0]
        def big_gemm(wname, cache_ap, rhs_sb, out_mode, out_sb=None, banks=None):
            psB8 = banks if banks is not None else psM8v
            ev_done = [None] * 8
            xs_ready = [0] * 16
            for dgrp in range(2):
                for dinc in range(ND):
                    slot = wsl_i[0] % 4
                    wsl_i[0] += 1
                    qq = "scalar" if slot % 2 == 0 else "sync"
                    slab = wsl[:, slot * 1024:(slot + 1) * 1024]
                    need_P((wname, dgrp, dinc))
                    wt(qq, "t", wsl_busy[slot])
                    swait(qq, PT[(wname, dgrp, dinc)])
                    tok = sdma(qq, "awr", slot, slab,
                               cache_ap[(dgrp * ND + dinc) * 128:
                                        (dgrp * ND + dinc + 1) * 128, :])
                    swait("tensor", tok)
                    if dinc == 0:
                        if dgrp == 0:
                            wt("tensor", "s", C["s"])
                            wt("tensor", "v", C["v"])
                        else:
                            for bk in range(8):
                                sm, vv = ev_done[bk]
                                wt("tensor", sm, vv)
                    st, sp = (dinc == 0), (dinc == ND - 1)

                    def mm8(e, s_=slab, d=dinc, st_=st, sp_=sp):
                        for bk in range(8):
                            ins = e.matmul(
                                out=psB8[bk][:, 0:512],
                                lhsT=s_[:, bk * 128:(bk + 1) * 128],
                                rhs=rhs_sb[:, d * 512:(d + 1) * 512],
                                start=st_, stop=sp_)
                            if bk == 7:
                                ins.then_inc(sem_t, 1)
                    em("tensor", mm8)
                    C["t"] += 1
                    wsl_busy[slot] = C["t"]
                    drain_P(1)
                for bk in range(8):
                    doutc = dgrp * 8 + bk
                    if out_mode == "scatter_v":
                        wt("vector", "t", C["t"])
                        vi(lambda e, bk_=bk, dc=doutc: e.tensor_copy(
                            out=out_sb[:].rearrange("p (g a k) -> p g a k", a=8, k=16)[
                                :, :, :, dc],
                            in_=psB8[bk_][:, 0:512].rearrange("p (g a) -> p g a", a=8)))
                        ev_done[bk] = ("v", C["v"])
                    else:
                        wt("vector", "t", C["t"])
                        vi(lambda e, bk_=bk, dc=doutc: e.tensor_tensor(
                            out=xs[:, dc * 512:(dc + 1) * 512],
                            in0=xs[:, dc * 512:(dc + 1) * 512],
                            in1=psB8[bk_][:, 0:512], op=ALU.add))
                        ev_done[bk] = ("v", C["v"])
                        xs_ready[doutc] = C["v"]
                drain_P(1)
            return xs_ready

        mid_ev = {}

        def attn_middle():
            base_v = C["v"]
            base_s = C["s"]
            wt("tensor", "v", base_v)
            wt("tensor", "s", base_s)
            for k in ("S", "V", "O", "T"):
                for par in (0, 1):
                    mid_ev[(k, par)] = base_v
            for sg in range(TM // 32):
                par = sg % 2
                bS = psS if par == 0 else psA0
                bV = psV0 if par == 0 else psA1
                bO = psV1 if par == 0 else psB0
                bT = psR if par == 0 else psB1
                sb_S = Sbd[:, par * 512:(par + 1) * 512]
                sb_V = VTs[:, par * 512:(par + 1) * 512]
                sb_o = otm[:, par * 512:(par + 1) * 512]
                wt("tensor", "v", mid_ev[("S", par)])

                def mm1x4(e, sg_=sg, b=bS):
                    for g in range(4):
                        gg = sg_ * 4 + g
                        ins = e.matmul(out=b[:, g * 128:(g + 1) * 128],
                                       lhsT=Kb[:, gg * 128:(gg + 1) * 128],
                                       rhs=Qb[:, gg * 128:(gg + 1) * 128],
                                       start=True, stop=True)
                        if g == 3:
                            ins.then_inc(sem_t, 1)
                em("tensor", mm1x4)
                C["t"] += 1
                t_mm1 = C["t"]
                wt("tensor", "v", mid_ev[("V", par)])

                def mm2x4(e, sg_=sg, b=bV):
                    for g in range(4):
                        gg = sg_ * 4 + g
                        ins = e.matmul(out=b[:, g * 128:(g + 1) * 128],
                                       lhsT=Vb[:, gg * 128:(gg + 1) * 128], rhs=ID_B,
                                       start=True, stop=True)
                        if g == 3:
                            ins.then_inc(sem_t, 1)
                em("tensor", mm2x4)
                C["t"] += 1
                t_mm2 = C["t"]
                wt("vector", "t", t_mm1)
                vi(lambda e, b=bS, d=sb_S: e.tensor_tensor(
                    out=d, in0=b[:, 0:512], in1=msk[:], op=ALU.mult))
                mid_ev[("S", par)] = C["v"]
                v_sbd = C["v"]
                wt("vector", "t", t_mm2)
                vi(lambda e, b=bV, d=sb_V: e.tensor_copy(out=d, in_=b[:, 0:512]))
                mid_ev[("V", par)] = C["v"]
                v_vts = C["v"]
                wt("tensor", "v", v_vts)
                wt("tensor", "v", mid_ev[("O", par)])

                def mm3x4(e, b=bO, dS=sb_S, dV=sb_V):
                    for g in range(4):
                        ins = e.matmul(out=b[:, g * 128:(g + 1) * 128],
                                       lhsT=dS[:, g * 128:(g + 1) * 128],
                                       rhs=dV[:, g * 128:(g + 1) * 128],
                                       start=True, stop=True)
                        if g == 3:
                            ins.then_inc(sem_t, 1)
                em("tensor", mm3x4)
                C["t"] += 1
                t_mm3 = C["t"]
                wt("vector", "t", t_mm3)
                vi(lambda e, b=bO, d=sb_o: e.tensor_copy(out=d, in_=b[:, 0:512]))
                mid_ev[("O", par)] = C["v"]
                v_otm = C["v"]
                wt("tensor", "v", v_otm)
                wt("tensor", "v", mid_ev[("T", par)])

                def mm4x4(e, b=bT, d=sb_o):
                    for g in range(4):
                        ins = e.matmul(out=b[:, g * 128:(g + 1) * 128],
                                       lhsT=d[:, g * 128:(g + 1) * 128], rhs=ID_B,
                                       start=True, stop=True)
                        if g == 3:
                            ins.then_inc(sem_t, 1)
                em("tensor", mm4x4)
                C["t"] += 1
                t_mm4 = C["t"]
                wt("vector", "t", t_mm4)
                vi(lambda e, sg_=sg, b=bT: e.tensor_copy(
                    out=ob[:].rearrange("p (h t) -> p h t", h=ND)[
                        :, :, sg_ * 32:(sg_ + 1) * 32
                    ].transpose([0, 2, 1]),
                    in_=b[:, 0:512].rearrange("p (t h) -> p t h", h=16)))
                mid_ev[("T", par)] = C["v"]
                drain_P(4)

        def ln2_quant(ready=None):
            ln_stats(ready)
            wt("tensor", "s", C["s"])
            wt("tensor", "v", C["v"])
            bcast_row(MU, psV0[:, 0:512], NEG_ROW)
            wt("scalar", "t", C["t"])
            wt("scalar", "v", C["v"])
            em("scalar", lambda e: e.copy(out=bc1[:], in_=psV0[:, 0:512]).then_inc(sem_s, 1))
            C["s"] += 1
            wt("vector", "s", C["s"])
            if ready is not None:
                wt("vector", "v", ready[0])
            vw()
            vi(lambda e: e.tensor_copy(out=acc[:], in_=xs[:, 0:512]))
            vi(lambda e: e.tensor_copy(out=acc2[:], in_=xs[:, 0:512]))
            for c in range(1, ND):
                if ready is not None:
                    wt("vector", "v", ready[c])
                vw()
                vi(lambda e, c_=c: e.tensor_tensor(
                    out=acc[:], in0=acc[:], in1=xs[:, c_ * 512:(c_ + 1) * 512], op=ALU.max))
                vi(lambda e, c_=c: e.tensor_tensor(
                    out=acc2[:], in0=acc2[:], in1=xs[:, c_ * 512:(c_ + 1) * 512], op=ALU.min))
            wt("tensor", "v", C["v"])
            for blk in range(4):
                em("tensor", lambda e, b_=blk: e.matmul(
                    out=psV1[:, 0:128], lhsT=acc[:, b_ * 128:(b_ + 1) * 128],
                    rhs=ID_F, start=True, stop=True))
                em("tensor", lambda e, b_=blk: e.matmul(
                    out=psV1[:, 128:256], lhsT=acc2[:, b_ * 128:(b_ + 1) * 128],
                    rhs=ID_F, start=True, stop=True).then_inc(sem_t, 1))
                C["t"] += 1
                wt("vector", "t", C["t"])
                vi(lambda e, b_=blk: e.tensor_reduce(
                    out=tmQ[:, 40 + b_:41 + b_], in_=psV1[:, 0:128], axis=AX.X,
                    op=ALU.max))
                vi(lambda e, b_=blk: e.tensor_reduce(
                    out=tmQ[:, 36 + b_:37 + b_], in_=psV1[:, 128:256], axis=AX.X,
                    op=ALU.min))
                wt("tensor", "v", C["v"])
            for blk in range(4):
                em("tensor", lambda e, b_=blk: e.matmul(
                    out=psV1[:, 256 + b_:257 + b_],
                    lhsT=RS[0:1, b_ * 128:(b_ + 1) * 128], rhs=ONE_1,
                    start=True, stop=True))
                em("tensor", lambda e, b_=blk: e.matmul(
                    out=psV1[:, 260 + b_:261 + b_],
                    lhsT=MU[0:1, b_ * 128:(b_ + 1) * 128], rhs=ONE_1,
                    start=True, stop=True))
            em("tensor", lambda e: e.matmul(
                out=psV1[0:1, 264:265], lhsT=ONE_ROW[:, 0:1], rhs=ONE_1,
                start=True, stop=True).then_inc(sem_t, 1))
            C["t"] += 1
            wt("vector", "t", C["t"])
            vw()
            vi(lambda e: e.tensor_copy(out=tmQ[:, 44:48], in_=psV1[:, 256:260]))
            vi(lambda e: e.tensor_copy(out=tmQ[:, 28:32], in_=psV1[:, 260:264]))
            vw()
            vi(lambda e: e.tensor_tensor(
                out=tmQ[:, 40:44], in0=tmQ[:, 40:44], in1=tmQ[:, 28:32], op=ALU.subtract))
            vi(lambda e: e.tensor_tensor(
                out=tmQ[:, 36:40], in0=tmQ[:, 28:32], in1=tmQ[:, 36:40], op=ALU.subtract))
            vw()
            vi(lambda e: e.tensor_tensor(
                out=tmQ[:, 40:44], in0=tmQ[:, 40:44], in1=tmQ[:, 36:40], op=ALU.max))
            vw()
            vi(lambda e: e.tensor_tensor(
                out=tmQ[:, 48:52], in0=tmQ[:, 40:44], in1=tmQ[:, 44:48], op=ALU.mult))
            vw()
            vi(lambda e: e.tensor_single_scalar(
                out=tmQ[:, 48:52], in_=tmQ[:, 48:52], scalar=EPS, op=ALU.max))
            vw()
            vi(lambda e: e.reciprocal(out=tmQ[:, 52:56], in_=tmQ[:, 48:52]))
            vw()
            vi(lambda e: e.tensor_tensor(
                out=tmQ[:, 52:56], in0=tmQ[:, 52:56], in1=tmQ[:, 44:48], op=ALU.mult))
            vw()
            vi(lambda e: e.tensor_single_scalar(
                out=tmQ[:, 52:56], in_=tmQ[:, 52:56], scalar=QMAX, op=ALU.mult))
            vi(lambda e: e.tensor_single_scalar(
                out=tmQ[:, 56:60], in_=tmQ[:, 48:52], scalar=1.0 / QMAX, op=ALU.mult))
            wt("tensor", "v", C["v"])
            for blk in range(4):
                em("tensor", lambda e, b_=blk: e.matmul(
                    out=psR[0:1, b_ * 128:(b_ + 1) * 128], lhsT=tmQ[:, 52 + b_:53 + b_],
                    rhs=ID_F, start=True, stop=True))
                em("tensor", lambda e, b_=blk: e.matmul(
                    out=psR[32:33, b_ * 128:(b_ + 1) * 128], lhsT=tmQ[:, 56 + b_:57 + b_],
                    rhs=ID_F, start=True, stop=True, tile_position=(0, 32)))
            em("tensor", lambda e: e.matmul(
                out=psR[64:65, 0:1], lhsT=ONE_ROW[:, 0:1], rhs=ONE_1,
                start=True, stop=True, tile_position=(0, 64)).then_inc(sem_t, 1))
            C["t"] += 1
            wt("vector", "t", C["t"])
            vi(lambda e: e.tensor_copy(out=WB, in_=psR[0:1, 0:512]))
            vi(lambda e: e.tensor_copy(out=WA, in_=psR[32:33, 0:512]))
            wt("tensor", "v", C["v"])
            bcast_row(WB, psV0[:, 0:512], ONE_ROW)
            wt("scalar", "t", C["t"])
            wt("scalar", "v", C["v"])
            em("scalar", lambda e: e.copy(out=bc0[:], in_=psV0[:, 0:512]).then_inc(sem_s, 1))
            C["s"] += 1
            wt("vector", "s", C["s"])
            for c in range(ND):
                wt("vector", "s", i8_copy[c % 3])
                swait("vector", qs_tok[c % 3])
                vi(lambda e, c_=c: e.tensor_tensor(
                    out=tmp1[:, (c_ % 2) * 512:(c_ % 2) * 512 + 512],
                    in0=xs[:, c_ * 512:(c_ + 1) * 512], in1=bc1[:], op=ALU.add))
                vw()
                vi(lambda e, c_=c: e.tensor_tensor(
                    out=i8t[:, (c_ % 3) * 512:(c_ % 3) * 512 + 512],
                    in0=tmp1[:, (c_ % 2) * 512:(c_ % 2) * 512 + 512],
                    in1=bc0[:], op=ALU.mult))
                wt("scalar", "v", C["v"])
                si(lambda e, c_=c: e.copy(
                    out=Ab[:, c_ * 512:(c_ + 1) * 512],
                    in_=i8t[:, (c_ % 3) * 512:(c_ % 3) * 512 + 512]))
                i8_copy[c % 3] = C["s"]
                drain_P(2)

        wb_busy1 = [0, 0, 0]
        wb_busy2 = [0, 0, 0]
        wb_conv1 = [0, 0, 0]
        wb_conv2 = [0, 0, 0]
        gl_last = [None, None]
        qn = [0]

        def ffn1(mt):
            need_P(("ws", 1))
            wt("tensor", "v", C["v"])
            wt("tensor", "s", C["s"])
            bcast_row(WA, psV0[:, 0:512], ONE_ROW)
            wt("vector", "t", C["t"])
            vw()
            vi(lambda e: e.tensor_scalar(
                out=bc0[:], in0=psV0[:, 0:512], scalar1=tmQ[:, 24:25], scalar2=None,
                op0=ALU.mult))
            psH_busy = [C["v"]] * 4
            Gb_busy_d = [gl_last[0], gl_last[1]]
            wt("scalar", "v", C["v"])

            def trig_quad(wname, cache_ap, wibuf, conv, wstream, p, q):
                slot = (qn[0] // 2) % 3
                dsti = wibuf[:, slot * 1024:(slot + 1) * 1024]
                qq = "sync" if wstream == "w1" else "scalar"
                need_P((wname, p, q))
                wt(qq, "s", conv[slot])
                swait(qq, PT[(wname, p, q)])
                tok = sdma(qq, wstream, slot, dsti,
                           cache_ap[(p * 4 + q) * 128:(p * 4 + q + 1) * 128, :])
                qn[0] += 1
                return slot, tok

            def conv_quad(wbuf, wibuf, busy, conv, slot, tok):
                dsti = wibuf[:, slot * 1024:(slot + 1) * 1024]
                dst = wbuf[:, slot * 1024:(slot + 1) * 1024]
                swait("scalar", tok)
                wt("scalar", "t", busy[slot])
                si(lambda e, d=dst, s=dsti: e.copy(out=d, in_=s))
                conv[slot] = C["s"]
                return dst

            for p in range(NJ // 2):
                for q in range(4):
                    sl1, t1 = trig_quad("w1", wq1c, wb1i, wb_conv1, "w1", p, q)
                    sl2, t2 = trig_quad("w2", wq2c, wb2i, wb_conv2, "w2", p, q)
                    s1 = conv_quad(wb1, wb1i, wb_busy1, wb_conv1, sl1, t1)
                    s2 = conv_quad(wb2, wb2i, wb_busy2, wb_conv2, sl2, t2)
                    wt("tensor", "s", max(wb_conv1[sl1], wb_conv2[sl2]))
                    if q == 0:
                        for bk in range(4):
                            wt("tensor", "v", psH_busy[bk])
                    for isub in range(4):
                        i = q * 4 + isub
                        st, sp = (i == 0), (i == ND - 1)

                        def mms(e, s1_=s1, s2_=s2, is_=isub, i_=i, st_=st, sp_=sp):
                            e.matmul(out=psA0[:, 0:512],
                                     lhsT=s1_[:, is_ * 256:is_ * 256 + 128],
                                     rhs=Ab[:, i_ * 512:(i_ + 1) * 512], start=st_, stop=sp_)
                            e.matmul(out=psA1[:, 0:512],
                                     lhsT=s1_[:, is_ * 256 + 128:is_ * 256 + 256],
                                     rhs=Ab[:, i_ * 512:(i_ + 1) * 512], start=st_, stop=sp_)
                            e.matmul(out=psB0[:, 0:512],
                                     lhsT=s2_[:, is_ * 256:is_ * 256 + 128],
                                     rhs=Ab[:, i_ * 512:(i_ + 1) * 512], start=st_, stop=sp_)
                            e.matmul(out=psB1[:, 0:512],
                                     lhsT=s2_[:, is_ * 256 + 128:is_ * 256 + 256],
                                     rhs=Ab[:, i_ * 512:(i_ + 1) * 512], start=st_,
                                     stop=sp_).then_inc(sem_t, 1)
                        em("tensor", mms)
                        C["t"] += 1
                    wb_busy1[sl1] = C["t"]
                    wb_busy2[sl2] = C["t"]
                drain_P(2)
                for half in range(2):
                    jc = 2 * p + half
                    psH1 = psA0 if half == 0 else psA1
                    psH2 = psB0 if half == 0 else psB1
                    gslot = jc % 2
                    gdst = Gbf[:, gslot * 512:(gslot + 1) * 512]
                    wt("vector", "t", C["t"])
                    vw()
                    vi(lambda e, p1=psH1: e.tensor_tensor(
                        out=tmp1[:, 0:512], in0=p1[:, 0:512], in1=bc0[:], op=ALU.mult))
                    wt("scalar", "v", C["v"])
                    swait("scalar", Gb_busy_d[gslot])
                    sw()
                    si(lambda e, g=gdst: e.activation(
                        out=g, in_=tmp1[:, 0:512], func=ACT.Silu))
                    wt("vector", "s", C["s"])
                    vi(lambda e, g=gdst, p2=psH2: e.tensor_tensor(
                        out=g, in0=g, in1=p2[:, 0:512], op=ALU.mult))
                    vw()
                    if half == 1:
                        vi(lambda e: e.tensor_tensor(
                            out=tmp1[:, 0:512], in0=Gbf[:, 0:512],
                            in1=Gbf[:, 512:1024], op=ALU.max))
                        vi(lambda e: e.tensor_tensor(
                            out=tmp1[:, 512:1024], in0=Gbf[:, 0:512],
                            in1=Gbf[:, 512:1024], op=ALU.min))
                        vw()
                        if p == 0:
                            vi(lambda e: e.tensor_copy(out=acc[:], in_=tmp1[:, 0:512]))
                            vi(lambda e: e.tensor_copy(out=acc2[:], in_=tmp1[:, 512:1024]))
                        else:
                            vi(lambda e: e.tensor_tensor(
                                out=acc[:], in0=acc[:], in1=tmp1[:, 0:512], op=ALU.max))
                            vi(lambda e: e.tensor_tensor(
                                out=acc2[:], in0=acc2[:], in1=tmp1[:, 512:1024], op=ALU.min))
                    if half == 0:
                        psH_busy[0] = C["v"]
                        psH_busy[2] = C["v"]
                    else:
                        psH_busy[1] = C["v"]
                        psH_busy[3] = C["v"]
                    wt("scalar", "v", C["v"])
                    Gb_busy_d[gslot] = sdma("scalar", "gs", gslot,
                        Gh[(jc * NMT + mt) * 128:(jc * NMT + mt + 1) * 128, :], gdst)

        def quant2_rows():
            need_P(("ws", 2))
            wt("tensor", "v", C["v"])
            wt("tensor", "s", C["s"])
            for blk in range(4):
                em("tensor", lambda e, b_=blk: e.matmul(
                    out=psV1[:, 0:128], lhsT=acc[:, b_ * 128:(b_ + 1) * 128],
                    rhs=ID_F, start=True, stop=True))
                em("tensor", lambda e, b_=blk: e.matmul(
                    out=psV1[:, 128:256], lhsT=acc2[:, b_ * 128:(b_ + 1) * 128],
                    rhs=ID_F, start=True, stop=True).then_inc(sem_t, 1))
                C["t"] += 1
                wt("vector", "t", C["t"])
                vi(lambda e, b_=blk: e.tensor_reduce(
                    out=tmQ[:, 40 + b_:41 + b_], in_=psV1[:, 0:128], axis=AX.X,
                    op=ALU.max))
                vi(lambda e, b_=blk: e.tensor_reduce(
                    out=tmQ[:, 36 + b_:37 + b_], in_=psV1[:, 128:256], axis=AX.X,
                    op=ALU.min))
                wt("tensor", "v", C["v"])
            vw()
            vi(lambda e: e.tensor_single_scalar(
                out=tmQ[:, 36:40], in_=tmQ[:, 36:40], scalar=-1.0, op=ALU.mult))
            vw()
            vi(lambda e: e.tensor_tensor(
                out=tmQ[:, 40:44], in0=tmQ[:, 40:44], in1=tmQ[:, 36:40], op=ALU.max))
            wt("tensor", "v", C["v"])
            for blk in range(4):
                em("tensor", lambda e, b_=blk: e.matmul(
                    out=psV1[:, 268 + b_:269 + b_],
                    lhsT=WA[0:1, b_ * 128:(b_ + 1) * 128], rhs=ONE_1,
                    start=True, stop=True))
            em("tensor", lambda e: e.matmul(
                out=psV1[0:1, 272:273], lhsT=ONE_ROW[:, 0:1], rhs=ONE_1,
                start=True, stop=True).then_inc(sem_t, 1))
            C["t"] += 1
            wt("vector", "t", C["t"])
            vi(lambda e: e.tensor_copy(out=tmQ[:, 20:24], in_=psV1[:, 268:272]))
            vw()
            vi(lambda e: e.tensor_scalar(
                out=tmQ[:, 20:24], in0=tmQ[:, 20:24], scalar1=tmQ[:, 25:26],
                scalar2=None, op0=ALU.mult))
            vw()
            vi(lambda e: e.tensor_tensor(
                out=tmQ[:, 48:52], in0=tmQ[:, 40:44], in1=tmQ[:, 20:24], op=ALU.mult))
            vw()
            vi(lambda e: e.tensor_single_scalar(
                out=tmQ[:, 48:52], in_=tmQ[:, 48:52], scalar=EPS, op=ALU.max))
            vw()
            vi(lambda e: e.reciprocal(out=tmQ[:, 52:56], in_=tmQ[:, 48:52]))
            vw()
            vi(lambda e: e.tensor_single_scalar(
                out=tmQ[:, 52:56], in_=tmQ[:, 52:56], scalar=QMAX, op=ALU.mult))
            vw()
            vi(lambda e: e.tensor_tensor(
                out=tmQ[:, 52:56], in0=tmQ[:, 52:56], in1=tmQ[:, 20:24], op=ALU.mult))
            vi(lambda e: e.tensor_single_scalar(
                out=tmQ[:, 56:60], in_=tmQ[:, 48:52], scalar=1.0 / QMAX, op=ALU.mult))
            wt("tensor", "v", C["v"])
            for blk in range(4):
                em("tensor", lambda e, b_=blk: e.matmul(
                    out=psR[0:1, b_ * 128:(b_ + 1) * 128], lhsT=tmQ[:, 52 + b_:53 + b_],
                    rhs=ID_F, start=True, stop=True))
                em("tensor", lambda e, b_=blk: e.matmul(
                    out=psR[32:33, b_ * 128:(b_ + 1) * 128], lhsT=tmQ[:, 56 + b_:57 + b_],
                    rhs=ID_F, start=True, stop=True, tile_position=(0, 32)))
            em("tensor", lambda e: e.matmul(
                out=psR[64:65, 0:1], lhsT=ONE_ROW[:, 0:1], rhs=ONE_1,
                start=True, stop=True, tile_position=(0, 64)).then_inc(sem_t, 1))
            C["t"] += 1
            wt("vector", "t", C["t"])
            vi(lambda e: e.tensor_copy(out=WB, in_=psR[0:1, 0:512]))
            vi(lambda e: e.tensor_copy(out=WA, in_=psR[32:33, 0:512]))
            wt("tensor", "v", C["v"])
            bcast_row(WB, psV0[:, 0:512], ONE_ROW)
            bcast_row(WA, psV1[:, 0:512], ONE_ROW)
            wt("scalar", "t", C["t"])
            wt("scalar", "v", C["v"])
            em("scalar", lambda e: e.copy(out=bc0[:], in_=psV0[:, 0:512]).then_inc(sem_s, 1))
            C["s"] += 1
            wt("vector", "t", C["t"])
            vw()
            vi(lambda e: e.tensor_scalar(
                out=bc1[:], in0=psV1[:, 0:512], scalar1=tmQ[:, 26:27], scalar2=None,
                op0=ALU.mult))

        w3s_busy = [0, 0, 0, 0]
        w3conv = [0, 0, 0, 0]
        i8_mult = [0, 0, 0]
        i8_copy = [0, 0, 0]
        qs_tok = [None, None, None]
        w3c_d = [None, None, None, None]
        qh_busy = [0, 0, 0]
        Gb_busy = [0, 0]

        def ffn2(mt):
            Gb_busy[0] = C["v"]
            Gb_busy[1] = C["v"]
            """two m-half passes (8 m-chunks = 8 full PSUM banks each) over all
            512 tokens; G reloaded and qH rebuilt per pass; w3 slabs
            [128 j, 1024 m] from the bf16 cache (fp32+quant+cache on mt0)."""
            for mh in range(2):
                wt("scalar", "t", C["t"])
                wt("scalar", "v", C["v"])
                psO_all = C["v"]
                wt("tensor", "v", psO_all)
                wt("vector", "s", C["s"])
                for jc in range(NJ):
                    islot = jc % 3
                    qslot = jc % 3
                    qdst = qHb[:, qslot * 512:(qslot + 1) * 512]
                    if mh == 0:
                        gslot = jc % 2
                        gdst = Gbf[:, gslot * 512:(gslot + 1) * 512]
                        wt("sync", "v", Gb_busy[gslot])
                        swait("sync", slast("gs", gslot))
                        gtok = sdma("sync", "gl", gslot, gdst,
                            Gh[(jc * NMT + mt) * 128:(jc * NMT + mt + 1) * 128, :])
                        gl_last[gslot] = gtok
                        swait("vector", gtok)
                        wt("vector", "s", i8_copy[islot])
                        swait("vector", qs_tok[islot])
                        vi(lambda e, g=gdst, i_=islot: e.tensor_tensor(
                            out=i8t[:, i_ * 512:(i_ + 1) * 512], in0=g, in1=bc0[:],
                            op=ALU.mult))
                        Gb_busy[gslot] = C["v"]
                        i8_mult[islot] = C["v"]
                        wt("scalar", "v", i8_mult[islot])
                        wt("scalar", "t", qh_busy[qslot])
                        si(lambda e, d=qdst, i_=islot: e.copy(
                            out=d, in_=i8t[:, i_ * 512:(i_ + 1) * 512]))
                        i8_copy[islot] = C["s"]
                        sw()
                        qs_tok[islot] = sdma("scalar", "qs", islot,
                            Qh8[jc * 128:(jc + 1) * 128, :],
                            i8t[:, islot * 512:(islot + 1) * 512])
                    else:
                        swait("sync", qs_tok[islot])
                        wt("sync", "s", i8_copy[islot])
                        rtok = sdma("sync", "qr", islot,
                            i8t[:, islot * 512:(islot + 1) * 512],
                            Qh8[jc * 128:(jc + 1) * 128, :])
                        swait("scalar", rtok)
                        wt("scalar", "t", qh_busy[qslot])
                        si(lambda e, d=qdst, i_=islot: e.copy(
                            out=d, in_=i8t[:, i_ * 512:(i_ + 1) * 512]))
                        i8_copy[islot] = C["s"]
                    slot = jc % 4
                    qw = "scalar" if slot % 2 == 0 else "sync"
                    slabi = w3i[:, slot * 1024:(slot + 1) * 1024]
                    slab = w3s[:, slot * 1024:(slot + 1) * 1024]
                    need_P(("w3", mh, jc))
                    wt(qw, "s", w3conv[slot])
                    swait(qw, PT[("w3", mh, jc)])
                    tok = sdma(qw, "w3s", slot, slabi,
                               wq3c[(mh * NJ + jc) * 128:(mh * NJ + jc + 1) * 128, :])
                    swait("scalar", tok)
                    wt("scalar", "t", w3s_busy[slot])
                    si(lambda e, d=slab, s=slabi: e.copy(out=d, in_=s))
                    w3conv[slot] = C["s"]
                    myconv = C["s"]
                    drain_P(1)
                    wt("tensor", "s", myconv)

                    def mms8(e, s_=slab, q_=qdst, j_=jc):
                        for bk in range(8):
                            ins = e.matmul(
                                out=psM8v[bk][:, 0:512],
                                lhsT=s_[:, bk * 128:(bk + 1) * 128],
                                rhs=q_, start=(j_ == 0), stop=(j_ == NJ - 1))
                            if bk == 7:
                                ins.then_inc(sem_t, 1)
                    em("tensor", mms8)
                    C["t"] += 1
                    w3s_busy[slot] = C["t"]
                    qh_busy[qslot] = C["t"]
                wt("vector", "t", C["t"])
                vw()
                mul_done = [0] * 8
                for bk in range(8):
                    if bk >= 2:
                        wt("vector", "v", add_done[bk - 2])
                    vi(lambda e, b_=bk: e.tensor_tensor(
                        out=tmp1[:, (b_ % 2) * 512:(b_ % 2) * 512 + 512],
                        in0=psM8v[b_][:, 0:512], in1=bc1[:], op=ALU.mult))
                    mul_done[bk] = C["v"]
                    if bk == 0:
                        add_done = [0] * 8
                        continue
                    bprev = bk - 1
                    wt("vector", "v", mul_done[bprev])
                    vi(lambda e, b_=bprev, mc_=mh * 8 + bprev: e.tensor_tensor(
                        out=xs[:, mc_ * 512:(mc_ + 1) * 512],
                        in0=xs[:, mc_ * 512:(mc_ + 1) * 512],
                        in1=tmp1[:, (b_ % 2) * 512:(b_ % 2) * 512 + 512], op=ALU.add))
                    add_done[bprev] = C["v"]
                wt("vector", "v", mul_done[7])
                vi(lambda e, mc_=mh * 8 + 7: e.tensor_tensor(
                    out=xs[:, mc_ * 512:(mc_ + 1) * 512],
                    in0=xs[:, mc_ * 512:(mc_ + 1) * 512],
                    in1=tmp1[:, 512:1024], op=ALU.add))
                add_done[7] = C["v"]

        sdma("sync", "xl", 0, csf[:], cf[:])
        sdma("sync", "xl", 0, csb[:], cbm[:])
        sdma("sync", "xl", 0, msk[:], cmask[:])
        vi(lambda e: e.memset(psS[:], 0.0))
        vi(lambda e: e.memset(psV0[:], 0.0))
        vi(lambda e: e.memset(psV1[:], 0.0))
        vi(lambda e: e.memset(accW[:], 0.0))

        drain_P(48)
        for mt in range(NMT):
            swait("sync", slast("out", 0))
            for c in range(ND):
                xtok = sdma("sync", "xl", 0, xs[:, c * 512:(c + 1) * 512],
                    xT[(c * NMT + mt) * 128:(c * NMT + mt + 1) * 128, :])
            swait("scalar", xtok)
            swait("tensor", xtok)
            swait("vector", xtok)
            ln1()
            drain_P(4)
            big_gemm("Wq", Wqc, Ab, "scatter_v", Qb)
            big_gemm("Wk", Wkc, Ab, "scatter_v", Kb)
            big_gemm("Wv", Wvc, Ab, "scatter_v", Vb)
            wt("tensor", "s", C["s"])
            attn_middle()
            xs_rdy = big_gemm("Wo", Woc, ob, "add_x",
                              banks=[psR, psA0, psA1, psB0, psB1, psS, psV0, psV1])
            ln2_quant(xs_rdy)
            ffn1(mt)
            quant2_rows()
            ffn2(mt)
            wt("scalar", "v", C["v"])
            for c in range(ND):
                sdma("scalar", "out", 0,
                    outT[(c * NMT + mt) * 128:(c * NMT + mt + 1) * 128, :],
                    xs[:, c * 512:(c + 1) * 512])

        for _nm in ("gs", "qs", "w3s", "c1", "c2", "c3"):
            for _sl in range(len(STREAMS[_nm][0])):
                swait("scalar", slast(_nm, _sl))
        swait("scalar", slast("out", 0))
        for _nm in ("xl", "st", "gl", "w1", "qr"):
            for _sl in range(len(STREAMS[_nm][0])):
                swait("sync", slast(_nm, _sl))
        for _nm in ("awc", "awr", "wfB", "awf", "w2"):
            for _sl in range(len(STREAMS[_nm][0])):
                swait("gpsimd", slast(_nm, _sl))
        wt("vector", "v", C["v"])
        wt("tensor", "t", C["t"])

        @block.sync
        def _(sync):
            for f in ops["sync"]:
                f(sync)

        @block.gpsimd
        def _(gpsimd):
            for f in ops["gpsimd"]:
                f(gpsimd)

        @block.vector
        def _(vector):
            for f in ops["vector"]:
                f(vector)

        @block.scalar
        def _(scalar):
            for f in ops["scalar"]:
                f(scalar)

        @block.tensor
        def _(tensor):
            for f in ops["tensor"]:
                f(tensor)

    return nc


_NC_CACHE = [None]


def tile_attn(w):
    return np.ascontiguousarray(
        w.reshape(ND, 128, 2, 1024).transpose(2, 0, 1, 3).reshape(2 * ND * 128, 1024))


def tile_ffn_in(wT):
    return np.ascontiguousarray(
        wT.reshape(4, 4, 128, NJ // 2, 256).transpose(3, 0, 2, 1, 4)
          .reshape((NJ // 2) * 4 * 128, 1024))


def tile_w3(w3T):
    return np.ascontiguousarray(
        w3T.reshape(NJ, 128, 2, 1024).transpose(2, 0, 1, 3).reshape(2 * NJ * 128, 1024))


def tile_x(xTb):
    return np.ascontiguousarray(
        xTb.reshape(ND, 128, NMT, TM).transpose(0, 2, 1, 3).reshape(ND * NMT * 128, TM))


def untile_out(o):
    return o.reshape(ND, NMT, 128, TM).transpose(0, 2, 1, 3).reshape(D, T)


def kernel(L0_attn_x, L0_attn_Wq, L0_attn_Wk, L0_attn_Wv, L0_attn_Wo,
           L0_ffn_w1, L0_ffn_w2, L0_ffn_w3):
    x = np.ascontiguousarray(np.asarray(L0_attn_x, dtype=np.float32))
    B = x.shape[0]
    cfv = np.zeros((128, 392), np.float32)
    cfv[:, 0:128] = np.eye(128, dtype=np.float32)
    cfv[:, 128] = 1.0
    cfv[0, 129:257] = 1.0
    cfv[0, 257:385] = -1.0
    cfv[0, 385] = 1e-5
    cbv = np.eye(128, dtype=ml_dtypes.bfloat16)
    cmv = np.tile(np.kron(np.eye(8), np.ones((16, 16))), (1, 4)).astype(ml_dtypes.bfloat16)

    if _NC_CACHE[0] is None:
        _NC_CACHE[0] = build()
    nc = _NC_CACHE[0]
    base = {
        "Wq": tile_attn(np.asarray(L0_attn_Wq, np.float32)),
        "Wk": tile_attn(np.asarray(L0_attn_Wk, np.float32)),
        "Wv": tile_attn(np.asarray(L0_attn_Wv, np.float32)),
        "Wo": tile_attn(np.asarray(L0_attn_Wo, np.float32)),
        "w1T": tile_ffn_in(np.asarray(L0_ffn_w1, np.float32).T),
        "w2T": tile_ffn_in(np.asarray(L0_ffn_w2, np.float32).T),
        "w3T": tile_w3(np.asarray(L0_ffn_w3, np.float32).T),
        "cf": cfv, "cbm": cbv, "cmask": cmv,
    }
    in_maps = [dict(base, xT=tile_x(x[b].T)) for b in range(B)]
    res = run_bass_kernel_spmd(nc, in_maps, core_ids=list(range(B)))
    out = np.stack([np.ascontiguousarray(untile_out(np.asarray(r["outT"])).T)
                    for r in res.results])
    return out.astype(np.float32)


if __name__ == "__main__":
    import importlib.util
    spec = importlib.util.spec_from_file_location("_ref", "/root/problem/reference.py")
    ref = importlib.util.module_from_spec(spec)
    spec.loader.exec_module(ref)
    ins = {k: np.asarray(v) for k, v in ref.setup_inputs().items()}
    out = kernel(**ins)
    exp = np.asarray(ref.reference(**ref.setup_inputs()))
    err = np.abs(out - exp)
    rvr = float(np.mean((out - exp) ** 2) / np.mean(exp ** 2))
    print("max abs err:", err.max(), "rvr:", rvr)


# version 23
# speedup vs baseline: 1.1809x; 1.1809x over previous
import contextlib
import numpy as np
import ml_dtypes
import concourse.bass as bass
import concourse.mybir as mybir
from concourse.bass_utils import run_bass_kernel_spmd

F32 = mybir.dt.float32
BF16 = mybir.dt.bfloat16
I8 = mybir.dt.int8
ALU = mybir.AluOpType
ACT = mybir.ActivationFunctionType
AX = mybir.AxisListType

D = 2048
ND = 16
T = 2048
TM = 512
NMT = T // TM
HID = 5632
NJ = HID // 128
CLIP_W = 1.4999999
EPS = 1e-5
QMAX = 127.0
NSLAB = 576


def build():
    nc = bass.Bass("TRN2", target_bir_lowering=False)
    xT = nc.dram_tensor("xT", [ND * NMT * 128, TM], F32, kind="ExternalInput").ap()
    Wq = nc.dram_tensor("Wq", [2 * ND * 128, 1024], F32, kind="ExternalInput").ap()
    Wk = nc.dram_tensor("Wk", [2 * ND * 128, 1024], F32, kind="ExternalInput").ap()
    Wv = nc.dram_tensor("Wv", [2 * ND * 128, 1024], F32, kind="ExternalInput").ap()
    Wo = nc.dram_tensor("Wo", [2 * ND * 128, 1024], F32, kind="ExternalInput").ap()
    w1T = nc.dram_tensor("w1T", [(NJ // 2) * 4 * 128, 1024], F32, kind="ExternalInput").ap()
    w2T = nc.dram_tensor("w2T", [(NJ // 2) * 4 * 128, 1024], F32, kind="ExternalInput").ap()
    w3T = nc.dram_tensor("w3T", [2 * NJ * 128, 1024], F32, kind="ExternalInput").ap()
    cf = nc.dram_tensor("cf", [128, 392], F32, kind="ExternalInput").ap()
    cbm = nc.dram_tensor("cbm", [128, 128], BF16, kind="ExternalInput").ap()
    cmask = nc.dram_tensor("cmask", [128, 512], BF16, kind="ExternalInput").ap()
    outT = nc.dram_tensor("outT", [ND * NMT * 128, TM], F32, kind="ExternalOutput").ap()
    Gh = nc.dram_tensor("Gh", [NJ * NMT * 128, TM], F32, kind="Internal").ap()
    Qh8 = nc.dram_tensor("Qh8", [HID, TM], I8, kind="Internal").ap()
    Wqc = nc.dram_tensor("Wqc", [2 * ND * 128, 1024], BF16, kind="Internal").ap()
    Wkc = nc.dram_tensor("Wkc", [2 * ND * 128, 1024], BF16, kind="Internal").ap()
    Wvc = nc.dram_tensor("Wvc", [2 * ND * 128, 1024], BF16, kind="Internal").ap()
    Woc = nc.dram_tensor("Woc", [2 * ND * 128, 1024], BF16, kind="Internal").ap()
    wq1c = nc.dram_tensor("wq1c", [(NJ // 2) * 4 * 128, 1024], BF16, kind="Internal").ap()
    wq2c = nc.dram_tensor("wq2c", [(NJ // 2) * 4 * 128, 1024], BF16, kind="Internal").ap()
    wq3c = nc.dram_tensor("wq3c", [2 * NJ * 128, 1024], BF16, kind="Internal").ap()

    ops = {k: [] for k in ("sync", "gpsimd", "vector", "scalar", "tensor")}
    C = {"t": 0, "v": 0, "s": 0}
    SEM = {}
    STREAMS = {}

    def em(eng, fn):
        ops[eng].append(fn)

    def wt(eng, sem, val):
        assert val <= C[sem], f"wait-on-future {sem}: {val} > {C[sem]}"
        if val <= 0:
            return
        em(eng, lambda e, s=sem, v=val: e.wait_ge(SEM[s], v))

    def vi(fn):
        C["v"] += 1
        em("vector", lambda e, f=fn: f(e).then_inc(SEM["v"], 1))

    def vw():
        wt("vector", "v", C["v"])

    def si(fn):
        C["s"] += 1
        em("scalar", lambda e, f=fn: f(e).then_inc(SEM["s"], 1))

    def sw():
        wt("scalar", "s", C["s"])

    def sdma(queue, stream, slot, out, in_):
        sems, cnt = STREAMS[stream]
        cnt[slot] += 16
        em(queue, lambda e, o=out, i=in_, sm=sems[slot]: e.dma_start(
            out=o, in_=i).then_inc(sm, 16))
        return (stream, slot, cnt[slot])

    def swait(eng, tok):
        if tok is None:
            return
        stream, slot, v = tok
        sems, _ = STREAMS[stream]
        em(eng, lambda e, sm=sems[slot], v_=v: e.wait_ge(sm, v_))

    def slast(stream, slot):
        sems, cnt = STREAMS[stream]
        return (stream, slot, cnt[slot]) if cnt[slot] > 0 else None

    with contextlib.ExitStack() as _st:
        def SB(name, shape, dt):
            return _st.enter_context(nc.sbuf_tensor(name, shape, dt))

        def PS(name, shape):
            return _st.enter_context(nc.psum_tensor(name, shape, F32))

        xs = SB("xs", [128, ND * TM], F32)
        Ab = SB("Ab", [128, ND * TM], BF16)
        Qb = SB("Qb", [128, ND * TM], BF16)
        Kb = SB("Kb", [128, ND * TM], BF16)
        Vb = SB("Vb", [128, ND * TM], BF16)
        ob = SB("ob", [128, ND * TM], BF16)
        Sbd = SB("Sbd", [128, 2 * 512], BF16)
        VTs = SB("VTs", [128, 2 * 512], BF16)
        otm = SB("otm", [128, 2 * 512], BF16)
        msk = SB("msk", [128, 512], BF16)
        qHb = SB("qHb", [128, 3 * 512], BF16)
        tmp1 = SB("tmp1", [128, 1024], F32)
        acc = SB("acc", [128, TM], F32)
        acc2 = SB("acc2", [128, TM], F32)
        bc0 = SB("bc0", [128, TM], F32)
        bc1 = SB("bc1", [128, TM], F32)
        rows = SB("rows", [1, 2048], F32)
        tmQ = SB("tmQ", [128, 64], F32)
        i8t = SB("i8t", [128, 3 * 512], I8)
        wsl = SB("wsl", [128, 4 * 1024], BF16)
        w3s = SB("w3s", [128, 2 * 2048], BF16)
        wfA = SB("wfA", [128, 2 * 1024], F32)
        wfB = SB("wfB", [128, 2 * 1024], F32)
        wi8 = SB("wi8", [128, 1024], I8)
        pcb = SB("pcb", [128, 2 * 1024], BF16)
        wb1 = SB("wb1", [128, 3 * 1024], BF16)
        wb2 = SB("wb2", [128, 3 * 1024], BF16)
        Gbf = SB("Gbf", [128, 2 * 512], F32)
        accW = SB("accW", [128, NSLAB], F32)
        csf = SB("csf", [128, 392], F32)
        csb = SB("csb", [128, 128], BF16)
        psA0 = PS("psA0", [128, 512])
        psA1 = PS("psA1", [128, 512])
        psB0 = PS("psB0", [128, 512])
        psB1 = PS("psB1", [128, 512])
        psS = PS("psS", [128, 512])
        psV0 = PS("psV0", [128, 512])
        psV1 = PS("psV1", [128, 512])
        psR = PS("psR", [128, 512])
        sem_x = _st.enter_context(nc.semaphore("sem_x"))
        sem_w = _st.enter_context(nc.semaphore("sem_w"))
        sem_d = _st.enter_context(nc.semaphore("sem_d"))
        sem_t = _st.enter_context(nc.semaphore("sem_t"))
        sem_v = _st.enter_context(nc.semaphore("sem_v"))
        sem_s = _st.enter_context(nc.semaphore("sem_s"))
        block = _st.enter_context(nc.Block())
        for _nm, _ns in (("xl", 1), ("st", 2), ("awf", 2), ("awr", 4), ("awc", 4), ("wfB", 2),
                         ("w1", 3), ("w2", 3), ("w3s", 4),
                         ("gl", 2), ("gs", 2), ("qs", 3), ("qr", 3), ("c1", 3), ("c2", 3), ("c3", 4),
                         ("out", 1)):
            STREAMS[_nm] = ([_st.enter_context(nc.semaphore(f"sm_{_nm}{_i}"))
                             for _i in range(_ns)], [0] * _ns)
        SEM.update(x=sem_x, w=sem_w, d=sem_d, t=sem_t, v=sem_v, s=sem_s)
        psM = [psA0, psA1, psB0, psB1]
        psM8v = [psA0, psA1, psB0, psB1, psS, psV0, psV1, psR]

        ID_F = csf[:, 0:128]
        ONE_COL = csf[:, 128:129]
        ONE_ROW = csf[0:1, 129:257]
        NEG_ROW = csf[0:1, 257:385]
        ONE_1 = csf[0:1, 128:129]
        ID_B = csb[:, 0:128]
        MU = rows[:, 0:512]
        RS = rows[:, 512:1024]
        WA = rows[:, 1024:1536]
        WB = rows[:, 1536:2048]

        prod_units = []
        prod_done = [0]
        PT = {}
        PIDX = {}
        wfA_busy = [0, 0]
        pcb_guard = [None, None]
        pfi = [0]

        def drain_P(n):
            while n > 0 and prod_done[0] < len(prod_units):
                prod_units[prod_done[0]]()
                prod_done[0] += 1
                n -= 1

        def need_P(key):
            idx = PIDX[key]
            while prod_done[0] <= idx:
                prod_units[prod_done[0]]()
                prod_done[0] += 1

        def mk_cast_unit(wname, wap, cache_ap, dgrp, dinc):
            def unit():
                fb = pfi[0] % 2
                pfi[0] += 1
                blk = dgrp * ND + dinc
                fdst = wfB[:, fb * 1024:(fb + 1) * 1024]
                gsem, gval = wfB_guard[fb]
                if gval > 0:
                    wt("gpsimd", gsem, gval)
                ftok = sdma("gpsimd", "awf", fb, fdst,
                            wap[blk * 128:(blk + 1) * 128, :])
                swait("scalar", ftok)
                swait("scalar", pcb_guard[fb])
                si(lambda e, d=pcb[:, fb * 1024:(fb + 1) * 1024], f=fdst: e.copy(
                    out=d, in_=f))
                wfB_guard[fb] = ("s", C["s"])
                sw()
                PT[(wname, dgrp, dinc)] = sdma(
                    "scalar", "awc", fb,
                    cache_ap[blk * 128:(blk + 1) * 128, :],
                    pcb[:, fb * 1024:(fb + 1) * 1024])
                pcb_guard[fb] = PT[(wname, dgrp, dinc)]
            PIDX[(wname, dgrp, dinc)] = len(prod_units)
            prod_units.append(unit)

        for _w, _wap, _cap in (("Wq", Wq, Wqc), ("Wk", Wk, Wkc),
                               ("Wv", Wv, Wvc), ("Wo", Wo, Woc)):
            for _dg in range(2):
                for _di in range(ND):
                    mk_cast_unit(_w, _wap, _cap, _dg, _di)

        stats_ranges = {}
        _scol = [0]

        def mk_stats(tag, wap, nrow, ncol):
            c0 = _scol[0]
            for r in range(nrow // 128):
                for cs in range(0, ncol, 1024):
                    cw = min(1024, ncol - cs)
                    ci = _scol[0]
                    _scol[0] += 1

                    def unit(r_=r, cs_=cs, cw_=cw, w_=wap, ci_=ci):
                        bb = ci_ % 2
                        dst = wfA[:, bb * 1024: bb * 1024 + cw_]
                        wt("sync", "s", wfA_busy[bb])
                        tok = sdma("sync", "st", bb, dst,
                                   w_[r_ * 128:(r_ + 1) * 128, cs_:cs_ + cw_])
                        swait("scalar", tok)
                        sw()
                        em("scalar", lambda e, d=dst, c2=ci_: e.activation(
                            out=d, in_=d, func=ACT.Abs,
                            accum_out=accW[:, c2:c2 + 1]).then_inc(sem_s, 1))
                        C["s"] += 1
                        wfA_busy[bb] = C["s"]
                    prod_units.append(unit)
            stats_ranges[tag] = (c0, _scol[0])

        mk_stats("w1", w1T, (NJ // 2) * 4 * 128, 1024)
        mk_stats("w2", w2T, (NJ // 2) * 4 * 128, 1024)
        mk_stats("w3", w3T, 2 * NJ * 128, 1024)
        assert _scol[0] <= NSLAB

        def emit_wscale(tag, wi, nelem):
            c0, c1 = stats_ranges[tag]
            wt("vector", "s", C["s"])
            vw()
            vi(lambda e: e.tensor_reduce(
                out=tmQ[:, 32 + wi:33 + wi], in_=accW[:, c0:c1], axis=AX.X,
                op=ALU.add))
            wt("tensor", "v", C["v"])
            wt("tensor", "s", C["s"])
            em("tensor", lambda e: e.matmul(
                out=psR[0:1, 480 + wi:481 + wi], lhsT=ONE_COL,
                rhs=tmQ[:, 32 + wi:33 + wi], start=True, stop=True).then_inc(sem_t, 1))
            C["t"] += 1
            wt("vector", "t", C["t"])
            vw()
            vi(lambda e: e.tensor_scalar(
                out=tmQ[0:1, 8 + wi:9 + wi], in0=psR[0:1, 480 + wi:481 + wi],
                scalar1=1.0 / nelem, scalar2=EPS, op0=ALU.mult, op1=ALU.max))
            vw()
            vi(lambda e: e.reciprocal(
                out=tmQ[0:1, 11 + wi:12 + wi], in_=tmQ[0:1, 8 + wi:9 + wi]))
            wt("tensor", "v", C["v"])
            em("tensor", lambda e: e.matmul(
                out=psR[:, 488 + 2 * wi:489 + 2 * wi], lhsT=ONE_ROW,
                rhs=tmQ[0:1, 11 + wi:12 + wi], start=True, stop=True))
            em("tensor", lambda e: e.matmul(
                out=psR[:, 489 + 2 * wi:490 + 2 * wi], lhsT=ONE_ROW,
                rhs=tmQ[0:1, 8 + wi:9 + wi], start=True, stop=True).then_inc(sem_t, 1))
            C["t"] += 1
            wt("scalar", "t", C["t"])
            em("scalar", lambda e: e.copy(out=tmQ[:, 16 + wi:17 + wi],
                                          in_=psR[:, 488 + 2 * wi:489 + 2 * wi]))
            em("scalar", lambda e: e.copy(out=tmQ[:, 24 + wi:25 + wi],
                                          in_=psR[:, 489 + 2 * wi:490 + 2 * wi]).then_inc(sem_s, 1))
            C["s"] += 1

        PIDX[("ws", 0)] = len(prod_units)
        prod_units.append(lambda: emit_wscale("w1", 0, D * HID))
        PIDX[("ws", 1)] = len(prod_units)
        prod_units.append(lambda: emit_wscale("w2", 1, D * HID))
        PIDX[("ws", 2)] = len(prod_units)
        prod_units.append(lambda: emit_wscale("w3", 2, HID * D))

        def mk_quad_quant(wname, wap, cache_ap, cstream, p, q, wi):
            def unit():
                fb = pfi[0] % 2
                pfi[0] += 1
                blk = p * 4 + q
                fdst = wfB[:, fb * 1024:(fb + 1) * 1024]
                gsem, gval = wfB_guard[fb]
                if gval > 0:
                    wt("gpsimd", gsem, gval)
                ftok = sdma("gpsimd", "awf", fb, fdst,
                            wap[blk * 128:(blk + 1) * 128, :])
                swait("vector", ftok)
                wt("vector", "s", C["s"])
                vw()
                vi(lambda e, f=fdst: e.tensor_scalar(
                    out=f, in0=f, scalar1=tmQ[:, 16 + wi:17 + wi], scalar2=CLIP_W,
                    op0=ALU.mult, op1=ALU.min))
                vw()
                vi(lambda e, f=fdst: e.tensor_scalar(
                    out=wi8[:], in0=f, scalar1=-CLIP_W, scalar2=None, op0=ALU.max))
                wfB_guard[fb] = ("v", C["v"])
                wt("scalar", "v", C["v"])
                swait("scalar", pcb_guard[fb])
                si(lambda e, d=pcb[:, fb * 1024:(fb + 1) * 1024]: e.copy(
                    out=d, in_=wi8[:]))
                sw()
                PT[(wname, p, q)] = sdma(
                    "scalar", cstream, fb,
                    cache_ap[blk * 128:(blk + 1) * 128, :],
                    pcb[:, fb * 1024:(fb + 1) * 1024])
                pcb_guard[fb] = PT[(wname, p, q)]
            PIDX[(wname, p, q)] = len(prod_units)
            prod_units.append(unit)

        def mk_w3_quant(mh, jc):
            def unit():
                fb = pfi[0] % 2
                pfi[0] += 1
                fdst = wfB[:, fb * 1024:(fb + 1) * 1024]
                gsem, gval = wfB_guard[fb]
                if gval > 0:
                    wt("gpsimd", gsem, gval)
                blk = mh * NJ + jc
                ftok = sdma("gpsimd", "awf", fb, fdst,
                            w3T[blk * 128:(blk + 1) * 128, :])
                swait("vector", ftok)
                wt("vector", "s", C["s"])
                vw()
                vi(lambda e, f=fdst: e.tensor_scalar(
                    out=f, in0=f, scalar1=tmQ[:, 18:19], scalar2=CLIP_W,
                    op0=ALU.mult, op1=ALU.min))
                vw()
                vi(lambda e, f=fdst: e.tensor_scalar(
                    out=wi8[:], in0=f, scalar1=-CLIP_W, scalar2=None, op0=ALU.max))
                wfB_guard[fb] = ("v", C["v"])
                wt("scalar", "v", C["v"])
                swait("scalar", pcb_guard[fb])
                si(lambda e, d=pcb[:, fb * 1024:(fb + 1) * 1024]: e.copy(
                    out=d, in_=wi8[:]))
                sw()
                PT[("w3", mh, jc)] = sdma(
                    "scalar", "c3", fb,
                    wq3c[blk * 128:(blk + 1) * 128, :],
                    pcb[:, fb * 1024:(fb + 1) * 1024])
                pcb_guard[fb] = PT[("w3", mh, jc)]
            PIDX[("w3", mh, jc)] = len(prod_units)
            prod_units.append(unit)

        for _p in range(NJ // 2):
            for _q in range(4):
                mk_quad_quant("w1", w1T, wq1c, "c1", _p, _q, 0)
                mk_quad_quant("w2", w2T, wq2c, "c2", _p, _q, 1)
        for _mh in range(2):
            for _jc in range(NJ):
                mk_w3_quant(_mh, _jc)

        def ln_stats(ready=None):
            wt("tensor", "s", C["s"])
            if ready is None:
                wt("tensor", "v", C["v"])
                wt("scalar", "v", C["v"])
            sq_done = [0]
            for pr in range(ND // 2):
                c0, c1 = 2 * pr, 2 * pr + 1
                wt("scalar", "t", sq_done[-1])
                if ready is not None:
                    wt("scalar", "v", max(ready[c0], ready[c1]))
                em("scalar", lambda e, c_=c0: e.activation(
                    out=tmp1[:], in_=xs[:, c_ * 512:(c_ + 2) * 512],
                    func=ACT.Square).then_inc(sem_s, 1))
                C["s"] += 1
                wt("tensor", "s", C["s"])
                if ready is not None:
                    wt("tensor", "v", max(ready[c0], ready[c1],
                                          ready[8] if pr == 0 else 0))
                st, sp = (pr == 0), (pr == ND // 2 - 1)

                def lnmm(e, c0_=c0, c1_=c1, st_=st, sp_=sp):
                    e.matmul(out=psR[0:1, 0:512], lhsT=ONE_COL,
                             rhs=xs[:, c0_ * 512:(c0_ + 1) * 512], start=st_, stop=False)
                    e.matmul(out=psR[0:1, 0:512], lhsT=ONE_COL,
                             rhs=xs[:, c1_ * 512:(c1_ + 1) * 512], start=False, stop=sp_)
                    e.matmul(out=psR[32:33, 0:512], lhsT=ONE_COL,
                             rhs=tmp1[:, 0:512], start=st_, stop=False,
                             tile_position=(0, 32))
                    e.matmul(out=psR[32:33, 0:512], lhsT=ONE_COL,
                             rhs=tmp1[:, 512:1024], start=False, stop=sp_,
                             tile_position=(0, 32)).then_inc(sem_t, 1)
                em("tensor", lnmm)
                C["t"] += 1
                sq_done.append(C["t"])
                drain_P(2)
            wt("vector", "t", C["t"])
            vw()
            vi(lambda e: e.tensor_single_scalar(
                out=MU, in_=psR[0:1, 0:512], scalar=1.0 / D, op=ALU.mult))
            vi(lambda e: e.tensor_single_scalar(
                out=WA, in_=psR[32:33, 0:512], scalar=1.0 / D, op=ALU.mult))
            vw()
            vi(lambda e: e.tensor_tensor(out=WB, in0=MU, in1=MU, op=ALU.mult))
            vw()
            vi(lambda e: e.tensor_tensor(out=WA, in0=WA, in1=WB, op=ALU.subtract))
            wt("scalar", "v", C["v"])
            sw()
            si(lambda e: e.activation(
                out=WB, in_=WA, func=ACT.Sqrt, bias=csf[0:1, 385:386]))
            wt("vector", "s", C["s"])
            vw()
            vi(lambda e: e.reciprocal(out=RS, in_=WB))

        def bcast_row(src_row, dst_ps, lhsT_row):
            em("tensor", lambda e, s=src_row, d=dst_ps, l=lhsT_row: e.matmul(
                out=d, lhsT=l, rhs=s, start=True, stop=True).then_inc(sem_t, 1))
            C["t"] += 1

        def ln1():
            ln_stats()
            wt("vector", "s", C["s"])
            vw()
            vi(lambda e: e.tensor_tensor(out=WB, in0=MU, in1=RS, op=ALU.mult))
            wt("tensor", "v", C["v"])
            wt("tensor", "s", C["s"])
            bcast_row(RS, psV0[:, 0:512], ONE_ROW)
            bcast_row(WB, psV1[:, 0:512], NEG_ROW)
            wt("scalar", "t", C["t"])
            wt("scalar", "v", C["v"])
            em("scalar", lambda e: e.copy(out=bc0[:], in_=psV0[:, 0:512]))
            em("scalar", lambda e: e.copy(out=bc1[:], in_=psV1[:, 0:512]).then_inc(sem_s, 1))
            C["s"] += 1
            wt("vector", "s", C["s"])
            for c in range(ND):
                vw()
                vi(lambda e, c_=c: e.tensor_tensor(
                    out=tmp1[:, 0:512], in0=xs[:, c_ * 512:(c_ + 1) * 512],
                    in1=bc0[:], op=ALU.mult))
                vw()
                vi(lambda e, c_=c: e.tensor_tensor(
                    out=Ab[:, c_ * 512:(c_ + 1) * 512], in0=tmp1[:, 0:512],
                    in1=bc1[:], op=ALU.add))

        wsl_busy = [0, 0, 0, 0]
        wfB_guard = [("s", 0), ("s", 0)]
        wsl_c = [None, None, None, None]
        wsl_i = [0]
        def big_gemm(wname, cache_ap, rhs_sb, out_mode, out_sb=None, banks=None):
            psB8 = banks if banks is not None else psM8v
            ev_done = [None] * 8
            xs_ready = [0] * 16
            for dgrp in range(2):
                for dinc in range(ND):
                    slot = wsl_i[0] % 4
                    wsl_i[0] += 1
                    qq = "scalar" if slot % 2 == 0 else "sync"
                    slab = wsl[:, slot * 1024:(slot + 1) * 1024]
                    need_P((wname, dgrp, dinc))
                    wt(qq, "t", wsl_busy[slot])
                    swait(qq, PT[(wname, dgrp, dinc)])
                    tok = sdma(qq, "awr", slot, slab,
                               cache_ap[(dgrp * ND + dinc) * 128:
                                        (dgrp * ND + dinc + 1) * 128, :])
                    swait("tensor", tok)
                    if dinc == 0:
                        if dgrp == 0:
                            wt("tensor", "s", C["s"])
                            wt("tensor", "v", C["v"])
                        else:
                            for bk in range(8):
                                sm, vv = ev_done[bk]
                                wt("tensor", sm, vv)
                    st, sp = (dinc == 0), (dinc == ND - 1)

                    def mm8(e, s_=slab, d=dinc, st_=st, sp_=sp):
                        for bk in range(8):
                            ins = e.matmul(
                                out=psB8[bk][:, 0:512],
                                lhsT=s_[:, bk * 128:(bk + 1) * 128],
                                rhs=rhs_sb[:, d * 512:(d + 1) * 512],
                                start=st_, stop=sp_)
                            if bk == 7:
                                ins.then_inc(sem_t, 1)
                    em("tensor", mm8)
                    C["t"] += 1
                    wsl_busy[slot] = C["t"]
                    drain_P(1)
                for bk in range(8):
                    doutc = dgrp * 8 + bk
                    if out_mode == "scatter_v":
                        wt("vector", "t", C["t"])
                        vi(lambda e, bk_=bk, dc=doutc: e.tensor_copy(
                            out=out_sb[:].rearrange("p (g a k) -> p g a k", a=8, k=16)[
                                :, :, :, dc],
                            in_=psB8[bk_][:, 0:512].rearrange("p (g a) -> p g a", a=8)))
                        ev_done[bk] = ("v", C["v"])
                    else:
                        wt("vector", "t", C["t"])
                        vi(lambda e, bk_=bk, dc=doutc: e.tensor_tensor(
                            out=xs[:, dc * 512:(dc + 1) * 512],
                            in0=xs[:, dc * 512:(dc + 1) * 512],
                            in1=psB8[bk_][:, 0:512], op=ALU.add))
                        ev_done[bk] = ("v", C["v"])
                        xs_ready[doutc] = C["v"]
                drain_P(1)
            return xs_ready

        mid_ev = {}

        def attn_middle():
            base_v = C["v"]
            base_s = C["s"]
            wt("tensor", "v", base_v)
            wt("tensor", "s", base_s)
            for k in ("S", "V", "O", "T"):
                for par in (0, 1):
                    mid_ev[(k, par)] = base_v
            for sg in range(TM // 32):
                par = sg % 2
                bS = psS if par == 0 else psA0
                bV = psV0 if par == 0 else psA1
                bO = psV1 if par == 0 else psB0
                bT = psR if par == 0 else psB1
                sb_S = Sbd[:, par * 512:(par + 1) * 512]
                sb_V = VTs[:, par * 512:(par + 1) * 512]
                sb_o = otm[:, par * 512:(par + 1) * 512]
                wt("tensor", "v", mid_ev[("S", par)])

                def mm1x4(e, sg_=sg, b=bS):
                    for g in range(4):
                        gg = sg_ * 4 + g
                        ins = e.matmul(out=b[:, g * 128:(g + 1) * 128],
                                       lhsT=Kb[:, gg * 128:(gg + 1) * 128],
                                       rhs=Qb[:, gg * 128:(gg + 1) * 128],
                                       start=True, stop=True)
                        if g == 3:
                            ins.then_inc(sem_t, 1)
                em("tensor", mm1x4)
                C["t"] += 1
                t_mm1 = C["t"]
                wt("tensor", "v", mid_ev[("V", par)])

                def mm2x4(e, sg_=sg, b=bV):
                    for g in range(4):
                        gg = sg_ * 4 + g
                        ins = e.matmul(out=b[:, g * 128:(g + 1) * 128],
                                       lhsT=Vb[:, gg * 128:(gg + 1) * 128], rhs=ID_B,
                                       start=True, stop=True)
                        if g == 3:
                            ins.then_inc(sem_t, 1)
                em("tensor", mm2x4)
                C["t"] += 1
                t_mm2 = C["t"]
                wt("vector", "t", t_mm1)
                vi(lambda e, b=bS, d=sb_S: e.tensor_tensor(
                    out=d, in0=b[:, 0:512], in1=msk[:], op=ALU.mult))
                mid_ev[("S", par)] = C["v"]
                v_sbd = C["v"]
                wt("vector", "t", t_mm2)
                vi(lambda e, b=bV, d=sb_V: e.tensor_copy(out=d, in_=b[:, 0:512]))
                mid_ev[("V", par)] = C["v"]
                v_vts = C["v"]
                wt("tensor", "v", v_vts)
                wt("tensor", "v", mid_ev[("O", par)])

                def mm3x4(e, b=bO, dS=sb_S, dV=sb_V):
                    for g in range(4):
                        ins = e.matmul(out=b[:, g * 128:(g + 1) * 128],
                                       lhsT=dS[:, g * 128:(g + 1) * 128],
                                       rhs=dV[:, g * 128:(g + 1) * 128],
                                       start=True, stop=True)
                        if g == 3:
                            ins.then_inc(sem_t, 1)
                em("tensor", mm3x4)
                C["t"] += 1
                t_mm3 = C["t"]
                wt("vector", "t", t_mm3)
                vi(lambda e, b=bO, d=sb_o: e.tensor_copy(out=d, in_=b[:, 0:512]))
                mid_ev[("O", par)] = C["v"]
                v_otm = C["v"]
                wt("tensor", "v", v_otm)
                wt("tensor", "v", mid_ev[("T", par)])

                def mm4x4(e, b=bT, d=sb_o):
                    for g in range(4):
                        ins = e.matmul(out=b[:, g * 128:(g + 1) * 128],
                                       lhsT=d[:, g * 128:(g + 1) * 128], rhs=ID_B,
                                       start=True, stop=True)
                        if g == 3:
                            ins.then_inc(sem_t, 1)
                em("tensor", mm4x4)
                C["t"] += 1
                t_mm4 = C["t"]
                wt("vector", "t", t_mm4)
                vi(lambda e, sg_=sg, b=bT: e.tensor_copy(
                    out=ob[:].rearrange("p (h t) -> p h t", h=ND)[
                        :, :, sg_ * 32:(sg_ + 1) * 32
                    ].transpose([0, 2, 1]),
                    in_=b[:, 0:512].rearrange("p (t h) -> p t h", h=16)))
                mid_ev[("T", par)] = C["v"]
                drain_P(4)

        def ln2_quant(ready=None):
            ln_stats(ready)
            wt("tensor", "s", C["s"])
            wt("tensor", "v", C["v"])
            bcast_row(MU, psV0[:, 0:512], NEG_ROW)
            wt("scalar", "t", C["t"])
            wt("scalar", "v", C["v"])
            em("scalar", lambda e: e.copy(out=bc1[:], in_=psV0[:, 0:512]).then_inc(sem_s, 1))
            C["s"] += 1
            wt("vector", "s", C["s"])
            if ready is not None:
                wt("vector", "v", ready[0])
            vw()
            vi(lambda e: e.tensor_copy(out=acc[:], in_=xs[:, 0:512]))
            vi(lambda e: e.tensor_copy(out=acc2[:], in_=xs[:, 0:512]))
            for c in range(1, ND):
                if ready is not None:
                    wt("vector", "v", ready[c])
                vw()
                vi(lambda e, c_=c: e.tensor_tensor(
                    out=acc[:], in0=acc[:], in1=xs[:, c_ * 512:(c_ + 1) * 512], op=ALU.max))
                vi(lambda e, c_=c: e.tensor_tensor(
                    out=acc2[:], in0=acc2[:], in1=xs[:, c_ * 512:(c_ + 1) * 512], op=ALU.min))
            wt("tensor", "v", C["v"])
            for blk in range(4):
                em("tensor", lambda e, b_=blk: e.matmul(
                    out=psV1[:, 0:128], lhsT=acc[:, b_ * 128:(b_ + 1) * 128],
                    rhs=ID_F, start=True, stop=True))
                em("tensor", lambda e, b_=blk: e.matmul(
                    out=psV1[:, 128:256], lhsT=acc2[:, b_ * 128:(b_ + 1) * 128],
                    rhs=ID_F, start=True, stop=True).then_inc(sem_t, 1))
                C["t"] += 1
                wt("vector", "t", C["t"])
                vi(lambda e, b_=blk: e.tensor_reduce(
                    out=tmQ[:, 40 + b_:41 + b_], in_=psV1[:, 0:128], axis=AX.X,
                    op=ALU.max))
                vi(lambda e, b_=blk: e.tensor_reduce(
                    out=tmQ[:, 36 + b_:37 + b_], in_=psV1[:, 128:256], axis=AX.X,
                    op=ALU.min))
                wt("tensor", "v", C["v"])
            for blk in range(4):
                em("tensor", lambda e, b_=blk: e.matmul(
                    out=psV1[:, 256 + b_:257 + b_],
                    lhsT=RS[0:1, b_ * 128:(b_ + 1) * 128], rhs=ONE_1,
                    start=True, stop=True))
                em("tensor", lambda e, b_=blk: e.matmul(
                    out=psV1[:, 260 + b_:261 + b_],
                    lhsT=MU[0:1, b_ * 128:(b_ + 1) * 128], rhs=ONE_1,
                    start=True, stop=True))
            em("tensor", lambda e: e.matmul(
                out=psV1[0:1, 264:265], lhsT=ONE_ROW[:, 0:1], rhs=ONE_1,
                start=True, stop=True).then_inc(sem_t, 1))
            C["t"] += 1
            wt("vector", "t", C["t"])
            vw()
            vi(lambda e: e.tensor_copy(out=tmQ[:, 44:48], in_=psV1[:, 256:260]))
            vi(lambda e: e.tensor_copy(out=tmQ[:, 28:32], in_=psV1[:, 260:264]))
            vw()
            vi(lambda e: e.tensor_tensor(
                out=tmQ[:, 40:44], in0=tmQ[:, 40:44], in1=tmQ[:, 28:32], op=ALU.subtract))
            vi(lambda e: e.tensor_tensor(
                out=tmQ[:, 36:40], in0=tmQ[:, 28:32], in1=tmQ[:, 36:40], op=ALU.subtract))
            vw()
            vi(lambda e: e.tensor_tensor(
                out=tmQ[:, 40:44], in0=tmQ[:, 40:44], in1=tmQ[:, 36:40], op=ALU.max))
            vw()
            vi(lambda e: e.tensor_tensor(
                out=tmQ[:, 48:52], in0=tmQ[:, 40:44], in1=tmQ[:, 44:48], op=ALU.mult))
            vw()
            vi(lambda e: e.tensor_single_scalar(
                out=tmQ[:, 48:52], in_=tmQ[:, 48:52], scalar=EPS, op=ALU.max))
            vw()
            vi(lambda e: e.reciprocal(out=tmQ[:, 52:56], in_=tmQ[:, 48:52]))
            vw()
            vi(lambda e: e.tensor_tensor(
                out=tmQ[:, 52:56], in0=tmQ[:, 52:56], in1=tmQ[:, 44:48], op=ALU.mult))
            vw()
            vi(lambda e: e.tensor_single_scalar(
                out=tmQ[:, 52:56], in_=tmQ[:, 52:56], scalar=QMAX, op=ALU.mult))
            vi(lambda e: e.tensor_single_scalar(
                out=tmQ[:, 56:60], in_=tmQ[:, 48:52], scalar=1.0 / QMAX, op=ALU.mult))
            wt("tensor", "v", C["v"])
            for blk in range(4):
                em("tensor", lambda e, b_=blk: e.matmul(
                    out=psR[0:1, b_ * 128:(b_ + 1) * 128], lhsT=tmQ[:, 52 + b_:53 + b_],
                    rhs=ID_F, start=True, stop=True))
                em("tensor", lambda e, b_=blk: e.matmul(
                    out=psR[32:33, b_ * 128:(b_ + 1) * 128], lhsT=tmQ[:, 56 + b_:57 + b_],
                    rhs=ID_F, start=True, stop=True, tile_position=(0, 32)))
            em("tensor", lambda e: e.matmul(
                out=psR[64:65, 0:1], lhsT=ONE_ROW[:, 0:1], rhs=ONE_1,
                start=True, stop=True, tile_position=(0, 64)).then_inc(sem_t, 1))
            C["t"] += 1
            wt("vector", "t", C["t"])
            vi(lambda e: e.tensor_copy(out=WB, in_=psR[0:1, 0:512]))
            vi(lambda e: e.tensor_copy(out=WA, in_=psR[32:33, 0:512]))
            wt("tensor", "v", C["v"])
            bcast_row(WB, psV0[:, 0:512], ONE_ROW)
            wt("scalar", "t", C["t"])
            wt("scalar", "v", C["v"])
            em("scalar", lambda e: e.copy(out=bc0[:], in_=psV0[:, 0:512]).then_inc(sem_s, 1))
            C["s"] += 1
            wt("vector", "s", C["s"])
            for c in range(ND):
                if c >= 3:
                    wt("vector", "s", i8_copy[c % 3])
                swait("vector", qs_tok[c % 3])
                vi(lambda e, c_=c: e.tensor_tensor(
                    out=tmp1[:, (c_ % 2) * 512:(c_ % 2) * 512 + 512],
                    in0=xs[:, c_ * 512:(c_ + 1) * 512], in1=bc1[:], op=ALU.add))
                vw()
                vi(lambda e, c_=c: e.tensor_tensor(
                    out=i8t[:, (c_ % 3) * 512:(c_ % 3) * 512 + 512],
                    in0=tmp1[:, (c_ % 2) * 512:(c_ % 2) * 512 + 512],
                    in1=bc0[:], op=ALU.mult))
                wt("scalar", "v", C["v"])
                si(lambda e, c_=c: e.copy(
                    out=Ab[:, c_ * 512:(c_ + 1) * 512],
                    in_=i8t[:, (c_ % 3) * 512:(c_ % 3) * 512 + 512]))
                i8_copy[c % 3] = C["s"]
                drain_P(2)

        wb_busy1 = [0, 0, 0]
        wb_busy2 = [0, 0, 0]
        wb1_d = [None, None, None]
        wb2_d = [None, None, None]
        gl_last = [None, None]
        qn = [0]

        def ffn1(mt):
            need_P(("ws", 1))
            wt("tensor", "v", C["v"])
            wt("tensor", "s", C["s"])
            bcast_row(WA, psV0[:, 0:512], ONE_ROW)
            wt("vector", "t", C["t"])
            vw()
            vi(lambda e: e.tensor_scalar(
                out=bc0[:], in0=psV0[:, 0:512], scalar1=tmQ[:, 24:25], scalar2=None,
                op0=ALU.mult))
            psH_busy = [C["v"]] * 4
            Gb_busy_d = [gl_last[0], gl_last[1]]
            wt("scalar", "v", C["v"])

            def load_quad(wname, cache_ap, wbuf, busy, wstream, p, q):
                slot = (qn[0] // 2) % 3
                dst = wbuf[:, slot * 1024:(slot + 1) * 1024]
                qq = "sync"
                need_P((wname, p, q))
                wt(qq, "t", busy[slot])
                swait(qq, PT[(wname, p, q)])
                tok = sdma(qq, wstream, slot, dst,
                           cache_ap[(p * 4 + q) * 128:(p * 4 + q + 1) * 128, :])
                swait("tensor", tok)
                qn[0] += 1
                return slot, dst

            for p in range(NJ // 2):
                for q in range(4):
                    sl1, s1 = load_quad("w1", wq1c, wb1, wb_busy1, "w1", p, q)
                    sl2, s2 = load_quad("w2", wq2c, wb2, wb_busy2, "w2", p, q)
                    if q == 0:
                        for bk in range(4):
                            wt("tensor", "v", psH_busy[bk])
                    for isub in range(4):
                        i = q * 4 + isub
                        st, sp = (i == 0), (i == ND - 1)

                        def mms(e, s1_=s1, s2_=s2, is_=isub, i_=i, st_=st, sp_=sp):
                            e.matmul(out=psA0[:, 0:512],
                                     lhsT=s1_[:, is_ * 256:is_ * 256 + 128],
                                     rhs=Ab[:, i_ * 512:(i_ + 1) * 512], start=st_, stop=sp_)
                            e.matmul(out=psA1[:, 0:512],
                                     lhsT=s1_[:, is_ * 256 + 128:is_ * 256 + 256],
                                     rhs=Ab[:, i_ * 512:(i_ + 1) * 512], start=st_, stop=sp_)
                            e.matmul(out=psB0[:, 0:512],
                                     lhsT=s2_[:, is_ * 256:is_ * 256 + 128],
                                     rhs=Ab[:, i_ * 512:(i_ + 1) * 512], start=st_, stop=sp_)
                            e.matmul(out=psB1[:, 0:512],
                                     lhsT=s2_[:, is_ * 256 + 128:is_ * 256 + 256],
                                     rhs=Ab[:, i_ * 512:(i_ + 1) * 512], start=st_,
                                     stop=sp_).then_inc(sem_t, 1)
                        em("tensor", mms)
                        C["t"] += 1
                    wb_busy1[sl1] = C["t"]
                    wb_busy2[sl2] = C["t"]
                drain_P(2)
                for half in range(2):
                    jc = 2 * p + half
                    psH1 = psA0 if half == 0 else psA1
                    psH2 = psB0 if half == 0 else psB1
                    gslot = jc % 2
                    gdst = Gbf[:, gslot * 512:(gslot + 1) * 512]
                    wt("vector", "t", C["t"])
                    vw()
                    vi(lambda e, p1=psH1: e.tensor_tensor(
                        out=tmp1[:, 0:512], in0=p1[:, 0:512], in1=bc0[:], op=ALU.mult))
                    wt("scalar", "v", C["v"])
                    swait("scalar", Gb_busy_d[gslot])
                    sw()
                    si(lambda e, g=gdst: e.activation(
                        out=g, in_=tmp1[:, 0:512], func=ACT.Silu))
                    wt("vector", "s", C["s"])
                    vi(lambda e, g=gdst, p2=psH2: e.tensor_tensor(
                        out=g, in0=g, in1=p2[:, 0:512], op=ALU.mult))
                    vw()
                    if half == 1:
                        vi(lambda e: e.tensor_tensor(
                            out=tmp1[:, 0:512], in0=Gbf[:, 0:512],
                            in1=Gbf[:, 512:1024], op=ALU.max))
                        vi(lambda e: e.tensor_tensor(
                            out=tmp1[:, 512:1024], in0=Gbf[:, 0:512],
                            in1=Gbf[:, 512:1024], op=ALU.min))
                        vw()
                        if p == 0:
                            vi(lambda e: e.tensor_copy(out=acc[:], in_=tmp1[:, 0:512]))
                            vi(lambda e: e.tensor_copy(out=acc2[:], in_=tmp1[:, 512:1024]))
                        else:
                            vi(lambda e: e.tensor_tensor(
                                out=acc[:], in0=acc[:], in1=tmp1[:, 0:512], op=ALU.max))
                            vi(lambda e: e.tensor_tensor(
                                out=acc2[:], in0=acc2[:], in1=tmp1[:, 512:1024], op=ALU.min))
                    if half == 0:
                        psH_busy[0] = C["v"]
                        psH_busy[2] = C["v"]
                    else:
                        psH_busy[1] = C["v"]
                        psH_busy[3] = C["v"]
                    wt("scalar", "v", C["v"])
                    Gb_busy_d[gslot] = sdma("scalar", "gs", gslot,
                        Gh[(jc * NMT + mt) * 128:(jc * NMT + mt + 1) * 128, :], gdst)

        def quant2_rows():
            need_P(("ws", 2))
            wt("tensor", "v", C["v"])
            wt("tensor", "s", C["s"])
            for blk in range(4):
                em("tensor", lambda e, b_=blk: e.matmul(
                    out=psV1[:, 0:128], lhsT=acc[:, b_ * 128:(b_ + 1) * 128],
                    rhs=ID_F, start=True, stop=True))
                em("tensor", lambda e, b_=blk: e.matmul(
                    out=psV1[:, 128:256], lhsT=acc2[:, b_ * 128:(b_ + 1) * 128],
                    rhs=ID_F, start=True, stop=True).then_inc(sem_t, 1))
                C["t"] += 1
                wt("vector", "t", C["t"])
                vi(lambda e, b_=blk: e.tensor_reduce(
                    out=tmQ[:, 40 + b_:41 + b_], in_=psV1[:, 0:128], axis=AX.X,
                    op=ALU.max))
                vi(lambda e, b_=blk: e.tensor_reduce(
                    out=tmQ[:, 36 + b_:37 + b_], in_=psV1[:, 128:256], axis=AX.X,
                    op=ALU.min))
                wt("tensor", "v", C["v"])
            vw()
            vi(lambda e: e.tensor_single_scalar(
                out=tmQ[:, 36:40], in_=tmQ[:, 36:40], scalar=-1.0, op=ALU.mult))
            vw()
            vi(lambda e: e.tensor_tensor(
                out=tmQ[:, 40:44], in0=tmQ[:, 40:44], in1=tmQ[:, 36:40], op=ALU.max))
            wt("tensor", "v", C["v"])
            for blk in range(4):
                em("tensor", lambda e, b_=blk: e.matmul(
                    out=psV1[:, 268 + b_:269 + b_],
                    lhsT=WA[0:1, b_ * 128:(b_ + 1) * 128], rhs=ONE_1,
                    start=True, stop=True))
            em("tensor", lambda e: e.matmul(
                out=psV1[0:1, 272:273], lhsT=ONE_ROW[:, 0:1], rhs=ONE_1,
                start=True, stop=True).then_inc(sem_t, 1))
            C["t"] += 1
            wt("vector", "t", C["t"])
            vi(lambda e: e.tensor_copy(out=tmQ[:, 20:24], in_=psV1[:, 268:272]))
            vw()
            vi(lambda e: e.tensor_scalar(
                out=tmQ[:, 20:24], in0=tmQ[:, 20:24], scalar1=tmQ[:, 25:26],
                scalar2=None, op0=ALU.mult))
            vw()
            vi(lambda e: e.tensor_tensor(
                out=tmQ[:, 48:52], in0=tmQ[:, 40:44], in1=tmQ[:, 20:24], op=ALU.mult))
            vw()
            vi(lambda e: e.tensor_single_scalar(
                out=tmQ[:, 48:52], in_=tmQ[:, 48:52], scalar=EPS, op=ALU.max))
            vw()
            vi(lambda e: e.reciprocal(out=tmQ[:, 52:56], in_=tmQ[:, 48:52]))
            vw()
            vi(lambda e: e.tensor_single_scalar(
                out=tmQ[:, 52:56], in_=tmQ[:, 52:56], scalar=QMAX, op=ALU.mult))
            vw()
            vi(lambda e: e.tensor_tensor(
                out=tmQ[:, 52:56], in0=tmQ[:, 52:56], in1=tmQ[:, 20:24], op=ALU.mult))
            vi(lambda e: e.tensor_single_scalar(
                out=tmQ[:, 56:60], in_=tmQ[:, 48:52], scalar=1.0 / QMAX, op=ALU.mult))
            wt("tensor", "v", C["v"])
            for blk in range(4):
                em("tensor", lambda e, b_=blk: e.matmul(
                    out=psR[0:1, b_ * 128:(b_ + 1) * 128], lhsT=tmQ[:, 52 + b_:53 + b_],
                    rhs=ID_F, start=True, stop=True))
                em("tensor", lambda e, b_=blk: e.matmul(
                    out=psR[32:33, b_ * 128:(b_ + 1) * 128], lhsT=tmQ[:, 56 + b_:57 + b_],
                    rhs=ID_F, start=True, stop=True, tile_position=(0, 32)))
            em("tensor", lambda e: e.matmul(
                out=psR[64:65, 0:1], lhsT=ONE_ROW[:, 0:1], rhs=ONE_1,
                start=True, stop=True, tile_position=(0, 64)).then_inc(sem_t, 1))
            C["t"] += 1
            wt("vector", "t", C["t"])
            vi(lambda e: e.tensor_copy(out=WB, in_=psR[0:1, 0:512]))
            vi(lambda e: e.tensor_copy(out=WA, in_=psR[32:33, 0:512]))
            wt("tensor", "v", C["v"])
            bcast_row(WB, psV0[:, 0:512], ONE_ROW)
            bcast_row(WA, psV1[:, 0:512], ONE_ROW)
            wt("scalar", "t", C["t"])
            wt("scalar", "v", C["v"])
            em("scalar", lambda e: e.copy(out=bc0[:], in_=psV0[:, 0:512]).then_inc(sem_s, 1))
            C["s"] += 1
            wt("vector", "t", C["t"])
            vw()
            vi(lambda e: e.tensor_scalar(
                out=bc1[:], in0=psV1[:, 0:512], scalar1=tmQ[:, 26:27], scalar2=None,
                op0=ALU.mult))

        w3s_busy = [0, 0, 0, 0]
        i8_mult = [0, 0, 0]
        i8_copy = [0, 0, 0]
        qs_tok = [None, None, None]
        w3c_d = [None, None, None, None]
        qh_busy = [0, 0, 0]
        Gb_busy = [0, 0]

        def ffn2(mt):
            Gb_busy[0] = C["v"]
            Gb_busy[1] = C["v"]
            i8_s_entry = list(i8_copy)
            """two m-half passes (8 m-chunks = 8 full PSUM banks each) over all
            512 tokens; G reloaded and qH rebuilt per pass; w3 slabs
            [128 j, 1024 m] from the bf16 cache (fp32+quant+cache on mt0)."""
            for mh in range(2):
                wt("scalar", "t", C["t"])
                wt("scalar", "v", C["v"])
                psO_all = C["v"]
                wt("tensor", "v", psO_all)
                wt("vector", "s", C["s"])
                for jc in range(NJ):
                    islot = jc % 3
                    qslot = jc % 3
                    qdst = qHb[:, qslot * 512:(qslot + 1) * 512]
                    if mh == 0:
                        gslot = jc % 2
                        gdst = Gbf[:, gslot * 512:(gslot + 1) * 512]
                        wt("sync", "v", Gb_busy[gslot])
                        swait("sync", slast("gs", gslot))
                        gtok = sdma("sync", "gl", gslot, gdst,
                            Gh[(jc * NMT + mt) * 128:(jc * NMT + mt + 1) * 128, :])
                        gl_last[gslot] = gtok
                        swait("vector", gtok)
                        if jc < 3:
                            wt("vector", "s", i8_s_entry[islot])
                        swait("vector", qs_tok[islot])
                        vi(lambda e, g=gdst, i_=islot: e.tensor_tensor(
                            out=i8t[:, i_ * 512:(i_ + 1) * 512], in0=g, in1=bc0[:],
                            op=ALU.mult))
                        Gb_busy[gslot] = C["v"]
                        i8_mult[islot] = C["v"]
                        vw()
                        wt("vector", "t", qh_busy[qslot])
                        vi(lambda e, d=qdst, i_=islot: e.tensor_copy(
                            out=d, in_=i8t[:, i_ * 512:(i_ + 1) * 512]))
                        i8_copy[islot] = C["v"]
                        wt("scalar", "v", i8_mult[islot])
                        qs_tok[islot] = sdma("scalar", "qs", islot,
                            Qh8[jc * 128:(jc + 1) * 128, :],
                            i8t[:, islot * 512:(islot + 1) * 512])
                    else:
                        swait("sync", qs_tok[islot])
                        wt("sync", "v", i8_copy[islot])
                        rtok = sdma("sync", "qr", islot,
                            i8t[:, islot * 512:(islot + 1) * 512],
                            Qh8[jc * 128:(jc + 1) * 128, :])
                        swait("vector", rtok)
                        wt("vector", "t", qh_busy[qslot])
                        vi(lambda e, d=qdst, i_=islot: e.tensor_copy(
                            out=d, in_=i8t[:, i_ * 512:(i_ + 1) * 512]))
                        i8_copy[islot] = C["v"]
                    qh_ready = C["v"]
                    slot = jc % 4
                    qw = "scalar" if slot % 2 == 0 else "sync"
                    slab = w3s[:, slot * 1024:(slot + 1) * 1024]
                    need_P(("w3", mh, jc))
                    wt(qw, "t", w3s_busy[slot])
                    swait(qw, PT[("w3", mh, jc)])
                    tok = sdma(qw, "w3s", slot, slab,
                               wq3c[(mh * NJ + jc) * 128:(mh * NJ + jc + 1) * 128, :])
                    swait("tensor", tok)
                    drain_P(1)
                    wt("tensor", "v", qh_ready)

                    def mms8(e, s_=slab, q_=qdst, j_=jc):
                        for bk in range(8):
                            ins = e.matmul(
                                out=psM8v[bk][:, 0:512],
                                lhsT=s_[:, bk * 128:(bk + 1) * 128],
                                rhs=q_, start=(j_ == 0), stop=(j_ == NJ - 1))
                            if bk == 7:
                                ins.then_inc(sem_t, 1)
                    em("tensor", mms8)
                    C["t"] += 1
                    w3s_busy[slot] = C["t"]
                    qh_busy[qslot] = C["t"]
                wt("vector", "t", C["t"])
                vw()
                mul_done = [0] * 8
                for bk in range(8):
                    if bk >= 2:
                        wt("vector", "v", add_done[bk - 2])
                    vi(lambda e, b_=bk: e.tensor_tensor(
                        out=tmp1[:, (b_ % 2) * 512:(b_ % 2) * 512 + 512],
                        in0=psM8v[b_][:, 0:512], in1=bc1[:], op=ALU.mult))
                    mul_done[bk] = C["v"]
                    if bk == 0:
                        add_done = [0] * 8
                        continue
                    bprev = bk - 1
                    wt("vector", "v", mul_done[bprev])
                    vi(lambda e, b_=bprev, mc_=mh * 8 + bprev: e.tensor_tensor(
                        out=xs[:, mc_ * 512:(mc_ + 1) * 512],
                        in0=xs[:, mc_ * 512:(mc_ + 1) * 512],
                        in1=tmp1[:, (b_ % 2) * 512:(b_ % 2) * 512 + 512], op=ALU.add))
                    add_done[bprev] = C["v"]
                wt("vector", "v", mul_done[7])
                vi(lambda e, mc_=mh * 8 + 7: e.tensor_tensor(
                    out=xs[:, mc_ * 512:(mc_ + 1) * 512],
                    in0=xs[:, mc_ * 512:(mc_ + 1) * 512],
                    in1=tmp1[:, 512:1024], op=ALU.add))
                add_done[7] = C["v"]

        sdma("sync", "xl", 0, csf[:], cf[:])
        sdma("sync", "xl", 0, csb[:], cbm[:])
        sdma("sync", "xl", 0, msk[:], cmask[:])
        vi(lambda e: e.memset(psS[:], 0.0))
        vi(lambda e: e.memset(psV0[:], 0.0))
        vi(lambda e: e.memset(psV1[:], 0.0))
        vi(lambda e: e.memset(accW[:], 0.0))

        drain_P(48)
        for mt in range(NMT):
            swait("sync", slast("out", 0))
            for c in range(ND):
                xtok = sdma("sync", "xl", 0, xs[:, c * 512:(c + 1) * 512],
                    xT[(c * NMT + mt) * 128:(c * NMT + mt + 1) * 128, :])
            swait("scalar", xtok)
            swait("tensor", xtok)
            swait("vector", xtok)
            ln1()
            drain_P(4)
            big_gemm("Wq", Wqc, Ab, "scatter_v", Qb)
            big_gemm("Wk", Wkc, Ab, "scatter_v", Kb)
            big_gemm("Wv", Wvc, Ab, "scatter_v", Vb)
            wt("tensor", "s", C["s"])
            attn_middle()
            xs_rdy = big_gemm("Wo", Woc, ob, "add_x",
                              banks=[psR, psA0, psA1, psB0, psB1, psS, psV0, psV1])
            ln2_quant(xs_rdy)
            ffn1(mt)
            quant2_rows()
            ffn2(mt)
            wt("scalar", "v", C["v"])
            for c in range(ND):
                sdma("scalar", "out", 0,
                    outT[(c * NMT + mt) * 128:(c * NMT + mt + 1) * 128, :],
                    xs[:, c * 512:(c + 1) * 512])

        for _nm in ("gs", "qs", "w3s", "c1", "c2", "c3"):
            for _sl in range(len(STREAMS[_nm][0])):
                swait("scalar", slast(_nm, _sl))
        swait("scalar", slast("out", 0))
        for _nm in ("xl", "st", "gl", "w1", "qr"):
            for _sl in range(len(STREAMS[_nm][0])):
                swait("sync", slast(_nm, _sl))
        for _nm in ("awc", "awr", "wfB", "awf", "w2"):
            for _sl in range(len(STREAMS[_nm][0])):
                swait("gpsimd", slast(_nm, _sl))
        wt("vector", "v", C["v"])
        wt("tensor", "t", C["t"])

        @block.sync
        def _(sync):
            for f in ops["sync"]:
                f(sync)

        @block.gpsimd
        def _(gpsimd):
            for f in ops["gpsimd"]:
                f(gpsimd)

        @block.vector
        def _(vector):
            for f in ops["vector"]:
                f(vector)

        @block.scalar
        def _(scalar):
            for f in ops["scalar"]:
                f(scalar)

        @block.tensor
        def _(tensor):
            for f in ops["tensor"]:
                f(tensor)

    return nc


_NC_CACHE = [None]


def tile_attn(w):
    return np.ascontiguousarray(
        w.reshape(ND, 128, 2, 1024).transpose(2, 0, 1, 3).reshape(2 * ND * 128, 1024))


def tile_ffn_in(wT):
    return np.ascontiguousarray(
        wT.reshape(4, 4, 128, NJ // 2, 256).transpose(3, 0, 2, 1, 4)
          .reshape((NJ // 2) * 4 * 128, 1024))


def tile_w3(w3T):
    return np.ascontiguousarray(
        w3T.reshape(NJ, 128, 2, 1024).transpose(2, 0, 1, 3).reshape(2 * NJ * 128, 1024))


def tile_x(xTb):
    return np.ascontiguousarray(
        xTb.reshape(ND, 128, NMT, TM).transpose(0, 2, 1, 3).reshape(ND * NMT * 128, TM))


def untile_out(o):
    return o.reshape(ND, NMT, 128, TM).transpose(0, 2, 1, 3).reshape(D, T)


def kernel(L0_attn_x, L0_attn_Wq, L0_attn_Wk, L0_attn_Wv, L0_attn_Wo,
           L0_ffn_w1, L0_ffn_w2, L0_ffn_w3):
    x = np.ascontiguousarray(np.asarray(L0_attn_x, dtype=np.float32))
    B = x.shape[0]
    cfv = np.zeros((128, 392), np.float32)
    cfv[:, 0:128] = np.eye(128, dtype=np.float32)
    cfv[:, 128] = 1.0
    cfv[0, 129:257] = 1.0
    cfv[0, 257:385] = -1.0
    cfv[0, 385] = 1e-5
    cbv = np.eye(128, dtype=ml_dtypes.bfloat16)
    cmv = np.tile(np.kron(np.eye(8), np.ones((16, 16))), (1, 4)).astype(ml_dtypes.bfloat16)

    if _NC_CACHE[0] is None:
        _NC_CACHE[0] = build()
    nc = _NC_CACHE[0]
    base = {
        "Wq": tile_attn(np.asarray(L0_attn_Wq, np.float32)),
        "Wk": tile_attn(np.asarray(L0_attn_Wk, np.float32)),
        "Wv": tile_attn(np.asarray(L0_attn_Wv, np.float32)),
        "Wo": tile_attn(np.asarray(L0_attn_Wo, np.float32)),
        "w1T": tile_ffn_in(np.asarray(L0_ffn_w1, np.float32).T),
        "w2T": tile_ffn_in(np.asarray(L0_ffn_w2, np.float32).T),
        "w3T": tile_w3(np.asarray(L0_ffn_w3, np.float32).T),
        "cf": cfv, "cbm": cbv, "cmask": cmv,
    }
    in_maps = [dict(base, xT=tile_x(x[b].T)) for b in range(B)]
    res = run_bass_kernel_spmd(nc, in_maps, core_ids=list(range(B)))
    out = np.stack([np.ascontiguousarray(untile_out(np.asarray(r["outT"])).T)
                    for r in res.results])
    return out.astype(np.float32)


if __name__ == "__main__":
    import importlib.util
    spec = importlib.util.spec_from_file_location("_ref", "/root/problem/reference.py")
    ref = importlib.util.module_from_spec(spec)
    spec.loader.exec_module(ref)
    ins = {k: np.asarray(v) for k, v in ref.setup_inputs().items()}
    out = kernel(**ins)
    exp = np.asarray(ref.reference(**ref.setup_inputs()))
    err = np.abs(out - exp)
    rvr = float(np.mean((out - exp) ** 2) / np.mean(exp ** 2))
    print("max abs err:", err.max(), "rvr:", rvr)
